# version 03
# baseline (speedup 1.0000x reference)
import numpy as np
import ml_dtypes
import concourse.bass as bass
import concourse.mybir as mybir
from contextlib import ExitStack
from concourse.bass_utils import run_bass_kernel_spmd

F32 = mybir.dt.float32
BF16 = mybir.dt.bfloat16
AF = mybir.ActivationFunctionType
OP = mybir.AluOpType
NPBF = ml_dtypes.bfloat16

D = 1024
CI = 1024
NS = 16
TC = 512
DK = 8
CC = 8
H4 = 4096
HM = 2048
NE = 6
EPS = 1e-5
EXPM_DEG = 14

DKTC = DK * TC
CFG = dict(NCA=16, NCF=8, taps=())

g_toks1 = [None] * 64
g_toks2 = [None] * 64


class Sem:
    def __init__(self, h):
        self.h = h
        self.n = 0

    def inc(self, instr, k=1):
        instr.then_inc(self.h, k)
        self.n += k

    def tok(self):
        return (self.h, self.n)


def wait(eng, tok):
    if tok is not None and tok[1] > 0:
        eng.wait_ge(tok[0], tok[1])


class PsumPool:
    def __init__(self):
        self.cons = {}

    def prod_wait(self, eng, tile):
        wait(eng, self.cons.get(id(tile)))

    def consumed(self, tile, tok):
        self.cons[id(tile)] = tok


def build_program(cfg):
    NCA = cfg["NCA"]
    NCF = cfg["NCF"]
    L = NCA * TC
    TOK = NCF * TC

    nc = bass.Bass("TRN2", target_bir_lowering=False)

    d_xT = nc.dram_tensor("xT", [NCA * 128, DKTC], F32, kind="ExternalInput")
    d_x0T = nc.dram_tensor("x0T", [NCF * 128, DKTC], F32, kind="ExternalInput")
    d_win = nc.dram_tensor("win", [128, DK * 2 * CI], F32, kind="ExternalInput")
    d_wout = nc.dram_tensor("wout", [128, CC * D], F32, kind="ExternalInput")
    d_convw = nc.dram_tensor("convw", [128, CC * 4], F32, kind="ExternalInput")
    d_convb = nc.dram_tensor("convb", [1, CI], F32, kind="ExternalInput")
    d_alog = nc.dram_tensor("alog", [NS, CI], F32, kind="ExternalInput")
    d_dp = nc.dram_tensor("dp", [1, CI], F32, kind="ExternalInput")
    d_lng = nc.dram_tensor("lng", [1, D], F32, kind="ExternalInput")
    d_lnb = nc.dram_tensor("lnb", [1, D], F32, kind="ExternalInput")
    d_aw1 = nc.dram_tensor("aw1", [8 * 128, 4096], F32, kind="ExternalInput")
    d_ab1 = nc.dram_tensor("ab1", [1, H4], F32, kind="ExternalInput")
    d_aw2 = nc.dram_tensor("aw2", [8 * 128, 4096], F32, kind="ExternalInput")
    d_ab2 = nc.dram_tensor("ab2", [1, D], F32, kind="ExternalInput")
    d_gatew = nc.dram_tensor("gatew", [128, DK * NE], F32, kind="ExternalInput")
    d_gateb = nc.dram_tensor("gateb", [1, NE], F32, kind="ExternalInput")
    d_ew1 = nc.dram_tensor("ew1", [NE, 4 * 128, 4096], F32, kind="ExternalInput")
    d_eb1 = nc.dram_tensor("eb1", [1, NE * HM], F32, kind="ExternalInput")
    d_ew2 = nc.dram_tensor("ew2", [NE, 8 * 128, 2048], F32, kind="ExternalInput")
    d_eb2 = nc.dram_tensor("eb2", [NE, D], F32, kind="ExternalInput")
    d_gens = nc.dram_tensor("gens", [128, DK * D], F32, kind="ExternalInput")
    d_gcw = nc.dram_tensor("gcw", [1, 2 * D], F32, kind="ExternalInput")
    d_gcb = nc.dram_tensor("gcb", [1, 1], F32, kind="ExternalInput")

    d_out = nc.dram_tensor("x2T", [NCF * 128, DKTC], F32, kind="ExternalOutput")

    d_aloc = nc.dram_tensor("aloc", [NCA * 128, DKTC], F32)
    d_ax = nc.dram_tensor("ax", [2 * NCF * 128, DKTC], F32)
    d_x1 = nc.dram_tensor("x1loc", [NCF * 128, DKTC], F32)
    d_n2 = nc.dram_tensor("n2loc", [NCF * 128, DKTC], F32)
    shb = nc.dram_tensor("shb", [2 * NCA * 128 * DKTC], F32, addr_space="Shared")
    d_xn2 = nc.dram_tensor("xn2loc", [NCF * 128, DKTC], BF16)
    d_ein = nc.dram_tensor("ein", [D, D], BF16)
    d_eout = nc.dram_tensor("eout", [8 * D, D], BF16)
    d_bin = nc.dram_tensor("barrin", [1, 8], F32)
    d_bout = nc.dram_tensor("barrout", [2, 8], F32)

    taps = {}
    for t in cfg.get("taps", ()):
        shp, dt = {
            "n1": ([128, DK * TC], BF16), "uc": ([128, CC * TC], BF16),
            "y": ([128, CC * TC], BF16), "yg": ([128, CC * TC], BF16),
            "R": ([128, DK * D], BF16), "ET": ([128, DK * D], BF16),
            "xn2": ([128, DK * TC], BF16), "abel": ([128, DK * TC], BF16),
            "acc": ([128, DK * TC], F32), "nonab": ([128, DK * TC], BF16),
            "wT6": ([NS, TC], F32), "x1c": ([128, DK * TC], F32),
            "apart": ([128, DK * TC], F32), "f": ([128, DK * TC], F32),
        }[t]
        taps[t] = nc.dram_tensor("dbg_" + t, shp, dt, kind="ExternalOutput")

    with ExitStack() as es_top:
        ones_bf_col = es_top.enter_context(nc.sbuf_tensor("ones_bf_col", [128, 1], BF16))
        ones_f_row = es_top.enter_context(nc.sbuf_tensor("ones_f_row", [1, 128], F32))
        ones_bf_row = es_top.enter_context(nc.sbuf_tensor("ones_bf_row", [1, 128], BF16))
        one11 = es_top.enter_context(nc.sbuf_tensor("one11", [1, 1], F32))
        ident_bf = es_top.enter_context(nc.sbuf_tensor("ident_bf", [128, 128], BF16))
        ident_f = es_top.enter_context(nc.sbuf_tensor("ident_f", [128, 128], F32))
        acols = es_top.enter_context(nc.sbuf_tensor("acols", [128, 128], F32))
        convwc = es_top.enter_context(nc.sbuf_tensor("convwc", [128, CC * 4], F32))
        convbc = es_top.enter_context(nc.sbuf_tensor("convbc", [128, CC], F32))
        dpc = es_top.enter_context(nc.sbuf_tensor("dpc", [128, CC], F32))
        lngc = es_top.enter_context(nc.sbuf_tensor("lngc", [128, DK], F32))
        lnbc = es_top.enter_context(nc.sbuf_tensor("lnbc", [128, DK], F32))
        ab1c = es_top.enter_context(nc.sbuf_tensor("ab1c", [128, H4 // 128], F32))
        ab2c = es_top.enter_context(nc.sbuf_tensor("ab2c", [128, DK], F32))
        eb1c = es_top.enter_context(nc.sbuf_tensor("eb1c", [128, NE * (HM // 128)], F32))
        gatewc = es_top.enter_context(nc.sbuf_tensor("gatewc", [128, DK * NE], BF16))
        gatebc = es_top.enter_context(nc.sbuf_tensor("gatebc", [NE, 1], F32))
        gcwc = es_top.enter_context(nc.sbuf_tensor("gcwc", [128, 16], BF16))
        gcbc = es_top.enter_context(nc.sbuf_tensor("gcbc", [1, 1], F32))
        Rmat = es_top.enter_context(nc.sbuf_tensor("Rmat", [128, DK * D], BF16))
        carry = es_top.enter_context(nc.sbuf_tensor("carry", [128, 128], F32))
        rows = es_top.enter_context(nc.sbuf_tensor("rows", [1, 5 * TC], F32))
        rowbf = es_top.enter_context(nc.sbuf_tensor("rowbf", [1, 2 * TC], BF16))
        ones_tc = es_top.enter_context(nc.sbuf_tensor("ones_tc", [1, TC], BF16))
        ab1row = es_top.enter_context(nc.sbuf_tensor("ab1row", [1, H4], BF16))
        ab2row = es_top.enter_context(nc.sbuf_tensor("ab2row", [1, D], BF16))
        gatebrow = es_top.enter_context(nc.sbuf_tensor("gatebrow", [1, NE], BF16))
        gcbrow = es_top.enter_context(nc.sbuf_tensor("gcbrow", [1, 1], BF16))
        eb1row = es_top.enter_context(nc.sbuf_tensor("eb1row", [1, HM], BF16))
        h_pe = es_top.enter_context(nc.semaphore("s_pe"))
        h_act = es_top.enter_context(nc.semaphore("s_act"))
        h_dve = es_top.enter_context(nc.semaphore("s_dve"))
        h_pool = es_top.enter_context(nc.semaphore("s_pool"))
        h_gdma = es_top.enter_context(nc.semaphore("s_gdma"))
        h_ldma = es_top.enter_context(nc.semaphore("s_ldma"))
        h_sdma = es_top.enter_context(nc.semaphore("s_sdma"))
        h_cc = es_top.enter_context(nc.semaphore("s_cc"))
        h_init = es_top.enter_context(nc.semaphore("s_init"))
        h_wst = es_top.enter_context(nc.semaphore("s_wst"))
        h_wst1 = es_top.enter_context(nc.semaphore("s_wst1"))
        h_xn = es_top.enter_context(nc.semaphore("s_xn"))
        h_eb = es_top.enter_context(nc.semaphore("s_eb"))
        block = es_top.enter_context(nc.Block())
        pe, act, dve, pool = Sem(h_pe), Sem(h_act), Sem(h_dve), Sem(h_pool)
        gdma, ldma, sdma, ccs = Sem(h_gdma), Sem(h_ldma), Sem(h_sdma), Sem(h_cc)
        init = Sem(h_init)
        wst = Sem(h_wst)
        wst1 = Sem(h_wst1)
        ebs = Sem(h_eb)
        xns = Sem(h_xn)
        pp = PsumPool()

        def emit_rows(t_sums, ps_s, ps_sq):
            m_r = rows[0:1, 0:TC]
            t1_r = rows[0:1, TC:2 * TC]
            t2_r = rows[0:1, 2 * TC:3 * TC]
            mr_r = rows[0:1, 3 * TC:4 * TC]

            def r1(v, t_sums=t_sums, ps_s=ps_s, ps_sq=ps_sq):
                wait(v, t_sums)
                v.tensor_scalar(out=m_r, in0=ps_s[:, :], scalar1=1.0 / D, scalar2=None, op0=OP.mult)
                i = v.tensor_scalar(out=t1_r, in0=ps_sq[:, :], scalar1=1.0 / D, scalar2=EPS,
                                    op0=OP.mult, op1=OP.add)
                dve.inc(i)
            block.vector(r1)
            tk = dve.tok()
            pp.consumed(ps_s, tk)
            pp.consumed(ps_sq, tk)

            def r2(v, tk=tk):
                wait(v, tk)
                i = v.tensor_tensor(out=t2_r, in0=m_r, in1=m_r, op=OP.mult)
                dve.inc(i)
            block.vector(r2)
            tk = dve.tok()

            def r3(v, tk=tk):
                wait(v, tk)
                i = v.tensor_tensor(out=t1_r, in0=t1_r, in1=t2_r, op=OP.subtract)
                dve.inc(i)
            block.vector(r3)
            tk = dve.tok()

            def r4(s, tk=tk):
                wait(s, tk)
                i = s.activation(out=t2_r, in_=t1_r, func=AF.Sqrt)
                act.inc(i)
            block.scalar(r4)
            tk = act.tok()

            def r5(v, tk=tk):
                wait(v, tk)
                i = v.reciprocal(out=t1_r, in_=t2_r)
                dve.inc(i)
            block.vector(r5)
            tk = dve.tok()

            def r6(v, tk=tk):
                wait(v, tk)
                i = v.scalar_tensor_tensor(out=mr_r, in0=m_r, scalar=-1.0, in1=t1_r,
                                           op0=OP.mult, op1=OP.mult)
                dve.inc(i)
            block.vector(r6)
            tk = dve.tok()

            def r7(v, tk=tk):
                wait(v, tk)
                v.tensor_copy(out=rowbf[0:1, 0:TC], in_=t1_r)
                i = v.tensor_copy(out=rowbf[0:1, TC:2 * TC], in_=mr_r)
                dve.inc(i)
            block.vector(r7)
            return dve.tok()

        def mm_ev(n, ps_list, emit_mm, emit_ev, ev_eng="scalar", head_pe=None, head_ev=None):
            esem = act if ev_eng == "scalar" else dve
            esec = block.scalar if ev_eng == "scalar" else block.vector
            for idx in range(n):
                def fm(t, idx=idx):
                    if idx == 0:
                        for tk in (head_pe or []):
                            wait(t, tk)
                    ps = ps_list[idx % len(ps_list)]
                    pp.prod_wait(t, ps)
                    mm = emit_mm(t, ps, idx)
                    pe.inc(mm)
                block.tensor(fm)
                tokm = pe.tok()

                def fe(e, idx=idx, tokm=tokm):
                    if idx == 0:
                        for tk in (head_ev or []):
                            wait(e, tk)
                    ps = ps_list[idx % len(ps_list)]
                    wait(e, tokm)
                    i = emit_ev(e, ps, idx)
                    esem.inc(i)
                    pp.consumed(ps, esem.tok())
                esec(fe)
            return esem.tok()

        with ExitStack() as es_pro:
            prow = es_pro.enter_context(nc.sbuf_tensor("prow", [1, NE * HM], F32))
            alog_sb = es_pro.enter_context(nc.sbuf_tensor("alog_sb", [NS, CI], F32))
            pcol = es_pro.enter_context(nc.sbuf_tensor("pcol", [128, 128], F32))
            pps = es_pro.enter_context(nc.psum_tensor("pps", [128, 128], F32))
            ppsB = es_pro.enter_context(nc.psum_tensor("ppsB", [128, 128], F32))
            pps6 = es_pro.enter_context(nc.psum_tensor("pps6", [NE, 1], F32))
            def c0(v):
                v.memset(ones_bf_col[:], 1.0)
                v.memset(ones_f_row[:], 1.0)
                v.memset(ones_bf_row[:], 1.0)
                v.memset(ones_tc[:], 1.0)
                v.memset(one11[:], 1.0)
                i = v.memset(ident_f[:], 0.0)
                dve.inc(i)
            block.vector(c0)
            t_zero = dve.tok()

            def g0(g):
                wait(g, t_zero)
                for p in range(128):
                    init.inc(g.dma_start(out=ident_f[p:p + 1, p:p + 1], in_=one11[0:1, 0:1]), 16)
            block.gpsimd(g0)
            t_identf = init.tok()

            def v1(v):
                wait(v, t_identf)
                i = v.tensor_copy(out=ident_bf[:], in_=ident_f[:])
                dve.inc(i)
            block.vector(v1)
            t_ident = dve.tok()

            rounds = [
                [(d_ab1, H4, ab1c)],
                [(d_ab2, D, ab2c), (d_convb, CI, convbc), (d_dp, CI, dpc),
                 (d_lng, D, lngc), (d_lnb, D, lnbc), (d_gcw, 2 * D, "gcw")],
                [(d_eb1, NE * HM, eb1c)],
            ]
            prev_copy = None
            prev_mm = None
            for ri in range(len(rounds)):
                def fload(s, ri=ri, prev_mm=prev_mm):
                    wait(s, prev_mm)
                    off = 0
                    for dr, ln, dst in rounds[ri]:
                        ldma.inc(s.dma_start(out=prow[0:1, off:off + ln], in_=dr.ap()[:, :]), 16)
                        off += ln
                block.sync(fload)
                t_load = ldma.tok()

                def fcolmm(t, ri=ri, t_load=t_load, prev_copy=prev_copy):
                    wait(t, t_load)
                    wait(t, t_zero)
                    wait(t, prev_copy)
                    off = 0
                    col = 0
                    mm = None
                    for dr, ln, dst in rounds[ri]:
                        for i in range(ln // 128):
                            mm = t.matmul(pps[:, col:col + 1],
                                          prow[0:1, off + i * 128:off + (i + 1) * 128],
                                          one11[:], start=True, stop=True)
                            col += 1
                        off += ln
                    pe.inc(mm)
                block.tensor(fcolmm)
                prev_mm = pe.tok()

                ncols_r = sum(ln // 128 for _, ln, _ in rounds[ri])

                def fcolevac(s, t_mm=prev_mm, ncols_r=ncols_r):
                    wait(s, t_mm)
                    i = s.copy(out=pcol[:, 0:ncols_r], in_=pps[:, 0:ncols_r])
                    act.inc(i)
                block.scalar(fcolevac)
                t_evac = act.tok()

                def fcolcopy(v, ri=ri, t_evac=t_evac):
                    wait(v, t_evac)
                    col = 0
                    last = None
                    for dr, ln, dst in rounds[ri]:
                        for i in range(ln // 128):
                            o = gcwc[:, i:i + 1] if dst == "gcw" else dst[:, i:i + 1]
                            last = v.tensor_copy(out=o, in_=pcol[:, col:col + 1])
                            col += 1
                    dve.inc(last)
                block.vector(fcolcopy)
                prev_copy = dve.tok()
            t_cols = prev_copy
            t_colmm_last = prev_mm

            def s2(s, t_colmm_last=t_colmm_last):
                wait(s, t_colmm_last)
                ldma.inc(s.dma_start(out=prow[0:1, 0:NE], in_=d_gateb.ap()[:, :]), 16)
                ldma.inc(s.dma_start(out=alog_sb[:, :], in_=d_alog.ap()[:, :]), 16)
                ldma.inc(s.dma_start(out=gcbc[:, :], in_=d_gcb.ap()[:, :]), 16)
                ldma.inc(s.dma_start(out=convwc[:, :], in_=d_convw.ap()[:, :]), 16)
            block.sync(s2)
            t_misc = ldma.tok()

            def g2(g):
                i = g.dma_start(out=gatewc[:, :], in_=d_gatew.ap()[:, :])
                gdma.inc(i, 16)
            block.gpsimd(g2)
            t_gwl = gdma.tok()

            def t3(t, t_misc=t_misc, t_cols=t_cols):
                wait(t, t_misc)
                wait(t, t_cols)
                i = t.matmul(pps6[:, :], prow[0:1, 0:NE], one11[:], start=True, stop=True)
                pe.inc(i)
                wait(t, t_ident)
                mm = None
                for j in range(CC):
                    mm = t.transpose(ppsB[:, j * NS:(j + 1) * NS],
                                     alog_sb[0:NS, j * 128:(j + 1) * 128], ident_f[0:NS, 0:NS])
                pe.inc(mm)
            block.tensor(t3)
            t_al = pe.tok()

            def s3(s, t_al=t_al):
                wait(s, t_al)
                s.copy(out=gatebc[:, :], in_=pps6[:, :])
                i = s.activation(out=pcol[:], in_=ppsB[:, :], func=AF.Exp)
                act.inc(i)
            block.scalar(s3)
            t_aexp = act.tok()

            def v3(v, t_aexp=t_aexp):
                wait(v, t_aexp)
                v.tensor_scalar(out=acols[:], in0=pcol[:], scalar1=-1.0, scalar2=None,
                                op0=OP.mult)
                i = v.memset(carry[:], 0.0)
                dve.inc(i)
            block.vector(v3)
            t_pro = dve.tok()

        with ExitStack() as es_exp:
            gsb = es_exp.enter_context(nc.sbuf_tensor("gsb", [128, DK * D], BF16))
            Sa = es_exp.enter_context(nc.sbuf_tensor("Sa", [128, DK * D], BF16))
            Sb = es_exp.enter_context(nc.sbuf_tensor("Sb", [128, DK * D], BF16))
            eps0 = es_exp.enter_context(nc.psum_tensor("eps0", [128, 512], F32))
            eps1 = es_exp.enter_context(nc.psum_tensor("eps1", [128, 512], F32))
            tps0 = es_exp.enter_context(nc.psum_tensor("tps0", [128, 128], BF16))
            tps1 = es_exp.enter_context(nc.psum_tensor("tps1", [128, 128], BF16))
            def gload(g, b1=(h_dve, None), ):
                wait(g, (h_dve, dve.n))
                wait(g, (h_act, act.n))
                wait(g, (h_pe, pe.n))
                i = g.dma_start(out=gsb[:, :], in_=d_gens.ap()[:, :])
                gdma.inc(i, 16)
            block.gpsimd(gload)
            t_g = gdma.tok()

            def vS(v, t_g=t_g, t_pro=t_pro):
                wait(v, t_g)
                wait(v, t_pro)
                i = v.tensor_scalar(out=Sa[:, :], in0=gsb[:, :], scalar1=1.0 / EXPM_DEG,
                                    scalar2=None, op0=OP.mult)
                dve.inc(i)
            block.vector(vS)
            tS0 = dve.tok()

            def vSd(v, tS0=tS0):
                wait(v, tS0)
                last = None
                for m in range(DK):
                    last = v.tensor_tensor(out=Sa[:, m * D + m * 128:m * D + (m + 1) * 128],
                                           in0=Sa[:, m * D + m * 128:m * D + (m + 1) * 128],
                                           in1=ident_bf[:], op=OP.add)
                dve.inc(last)
            block.vector(vSd)
            t_prev_dve = dve.tok()

            cur, nxt = Sa, Sb
            for step in range(EXPM_DEG - 1, 0, -1):
                def emm(t, ps, idx, cur=cur):
                    m, half = divmod(idx, 2)
                    mm = None
                    for kc in range(DK):
                        mm = t.matmul(ps[:, :],
                                      gsb[:, kc * D + m * 128:kc * D + (m + 1) * 128],
                                      cur[:, kc * D + half * 512:kc * D + (half + 1) * 512],
                                      start=(kc == 0), stop=(kc == DK - 1))
                    return mm

                def eev(v, ps, idx, nxt=nxt, step=step):
                    m, half = divmod(idx, 2)
                    return v.tensor_scalar(
                        out=nxt[:, m * D + half * 512:m * D + (half + 1) * 512],
                        in0=ps[:, :], scalar1=1.0 / step, scalar2=None, op0=OP.mult)
                t_evac = mm_ev(2 * DK, [eps0, eps1], emm, eev, ev_eng="vector",
                               head_pe=[t_prev_dve])

                def vdg(v, nxt=nxt, t_evac=t_evac):
                    wait(v, t_evac)
                    last = None
                    for m in range(DK):
                        last = v.tensor_tensor(out=nxt[:, m * D + m * 128:m * D + (m + 1) * 128],
                                               in0=nxt[:, m * D + m * 128:m * D + (m + 1) * 128],
                                               in1=ident_bf[:], op=OP.add)
                    dve.inc(last)
                block.vector(vdg)
                t_prev_dve = dve.tok()
                cur, nxt = nxt, cur

            if "ET" in taps:
                def tapet(s, cur=cur, tk=t_prev_dve):
                    wait(s, tk)
                    i = s.dma_start(out=taps["ET"].ap()[:, :], in_=cur[:, :])
                    sdma.inc(i, 16)
                block.sync(tapet)

            def eship(s, cur=cur, tk=t_prev_dve):
                wait(s, tk)
                i = s.dma_start(out=bass.AP(d_ein, 0, [[D, 128], [128 * D, DK], [1, D]]),
                                in_=cur[:, :])
                sdma.inc(i, 16)
            block.sync(eship)
            t_es = sdma.tok()

            def gcoll(g, t_es=t_es):
                wait(g, t_es)
                i = g.collective_compute("AllGather", OP.bypass,
                                         replica_groups=[list(range(8))],
                                         ins=[d_ein.ap().opt()],
                                         outs=[d_eout.ap().opt()])
                ccs.inc(i, 1)
                wait(g, ccs.tok())
                i = g.dma_start(out=gsb[:, :],
                                in_=bass.AP(d_eout, 3 * D * D, [[D, 128], [128 * D, DK], [1, D]]))
                gdma.inc(i, 16)
            block.gpsimd(gcoll)
            t_e3 = gdma.tok()

            def ttm(t, ps, idx):
                rc, c2 = divmod(idx, DK)
                return t.transpose(ps[:, :], gsb[:, c2 * D + rc * 128:c2 * D + (rc + 1) * 128],
                                   ident_bf[:])

            def tte(s, ps, idx, dst=nxt):
                rc, c2 = divmod(idx, DK)
                return s.activation(out=dst[:, rc * D + c2 * 128:rc * D + (c2 + 1) * 128],
                                    in_=ps[:, :], func=AF.Copy)
            t_T = mm_ev(DK * DK, [tps0, tps1], ttm, tte, head_pe=[t_e3])

            src = nxt
            other = cur
            t_chain = t_T
            for sidx in (2, 1, 0):
                def gld(g, sidx=sidx, t_chain=t_chain):
                    wait(g, t_chain)
                    i = g.dma_start(out=gsb[:, :],
                                    in_=bass.AP(d_eout, sidx * D * D,
                                                [[D, 128], [128 * D, DK], [1, D]]))
                    gdma.inc(i, 16)
                block.gpsimd(gld)
                t_lw = gdma.tok()

                dst_t = Rmat if sidx == 0 else other

                def pmm(t, ps, idx, src=src):
                    m, half = divmod(idx, 2)
                    mm = None
                    for kc in range(DK):
                        mm = t.matmul(ps[:, :],
                                      gsb[:, kc * D + m * 128:kc * D + (m + 1) * 128],
                                      src[:, kc * D + half * 512:kc * D + (half + 1) * 512],
                                      start=(kc == 0), stop=(kc == DK - 1))
                    return mm

                def pev(s, ps, idx, dst_t=dst_t):
                    m, half = divmod(idx, 2)
                    return s.activation(
                        out=dst_t[:, m * D + half * 512:m * D + (half + 1) * 512],
                        in_=ps[:, :], func=AF.Copy)
                t_chain = mm_ev(2 * DK, [eps0, eps1], pmm, pev, head_pe=[t_lw])
                src, other = dst_t, src
            t_R = t_chain

            if "R" in taps:
                def tapr(s, t_R=t_R):
                    wait(s, t_R)
                    i = s.dma_start(out=taps["R"].ap()[:, :], in_=Rmat[:, :])
                    sdma.inc(i, 16)
                block.sync(tapr)

        with ExitStack() as es_att:
            win_sb = es_att.enter_context(nc.sbuf_tensor("win_sb", [128, DK * 2 * CI], BF16))
            wout_sb = es_att.enter_context(nc.sbuf_tensor("wout_sb", [128, CC * D], BF16))
            xb0 = es_att.enter_context(nc.sbuf_tensor("xb0", [128, DK * TC], BF16))
            xb1 = es_att.enter_context(nc.sbuf_tensor("xb1", [128, DK * TC], BF16))
            n1a = es_att.enter_context(nc.sbuf_tensor("n1a", [128, DK * TC], BF16))
            n1b = es_att.enter_context(nc.sbuf_tensor("n1b", [128, DK * TC], BF16))
            u_raw = es_att.enter_context(nc.sbuf_tensor("u_raw", [128, CC * (TC + 3)], BF16))
            ucb = es_att.enter_context(nc.sbuf_tensor("ucb", [128, CC * TC], BF16))
            cvt0 = es_att.enter_context(nc.sbuf_tensor("cvt0", [128, 4 * TC], BF16))
            szb = es_att.enter_context(nc.sbuf_tensor("szb", [128, CC * TC], BF16))
            scano = es_att.enter_context(nc.sbuf_tensor("scano", [128, 8 * TC], BF16))
            pairb = es_att.enter_context(nc.sbuf_tensor("pairb", [128, 4 * TC], BF16))
            quadb = es_att.enter_context(nc.sbuf_tensor("quadb", [128, 3 * TC], BF16))
            ybuf = es_att.enter_context(nc.sbuf_tensor("ybuf", [128, CC * TC], BF16))
            ygbuf = es_att.enter_context(nc.sbuf_tensor("ygbuf", [128, CC * TC], BF16))
            apart = es_att.enter_context(nc.sbuf_tensor("apart", [128, DK * TC], F32))
            bcastt = es_att.enter_context(nc.sbuf_tensor("bcastt", [128, 2 * TC], BF16))
            sps0 = es_att.enter_context(nc.psum_tensor("sps0", [1, TC], F32))
            sps1 = es_att.enter_context(nc.psum_tensor("sps1", [1, TC], F32))
            bps = es_att.enter_context(nc.psum_tensor("bps", [128, TC], F32))
            wps0 = es_att.enter_context(nc.psum_tensor("wps0", [128, TC], F32))
            wps1 = es_att.enter_context(nc.psum_tensor("wps1", [128, TC], F32))
            ops0 = es_att.enter_context(nc.psum_tensor("ops0", [128, TC], F32))
            ops1 = es_att.enter_context(nc.psum_tensor("ops1", [128, TC], F32))
            def wload(g, bar=None):
                wait(g, (h_sdma, sdma.n))
                wait(g, (h_dve, dve.n))
                wait(g, (h_act, act.n))
                wait(g, (h_pe, pe.n))
                gdma.inc(g.dma_start(out=win_sb[:, :], in_=d_win.ap()[:, :]), 16)
                gdma.inc(g.dma_start(out=wout_sb[:, :], in_=d_wout.ap()[:, :]), 16)
            block.gpsimd(wload)
            t_w = gdma.tok()

            def uzero(v, bar=None):
                wait(v, (h_sdma, sdma.n))
                wait(v, (h_act, act.n))
                wait(v, (h_pe, pe.n))
                i = v.memset(u_raw[:, :], 0.0)
                dve.inc(i)
            block.vector(uzero)
            t_uzero = dve.tok()

            xbufs = [xb0, xb1]
            n1bufs = [n1a, n1b]
            t_xload = [None] * (NCA + 1)
            xb_free = [None, None]
            n1_free = [None, None]
            t_halo = [None] * (NCA + 1)
            t_halo[0] = t_uzero
            t_yg = [None] * NCA
            t_apart_dma = [None] * NCA

            def emit_xload(k):
                def f(g, k=k, ebar=(h_act, act.n), gprev=(h_gdma, gdma.n)):
                    wait(g, xb_free[k % 2])
                    if k < 2:
                        wait(g, ebar)
                        wait(g, (h_pe, pe.n) if k < 2 else None)
                        wait(g, gprev)
                    i = g.dma_start(out=xbufs[k % 2][:, :],
                                    in_=bass.AP(d_xT, k * 128 * DKTC, [[DKTC, 128], [1, DKTC]]))
                    gdma.inc(i, 16)
                block.gpsimd(f)
                t_xload[k] = gdma.tok()

            emit_xload(0)

            for k in range(NCA):
                xb = xbufs[k % 2]
                n1 = n1bufs[k % 2]
                if k + 1 < NCA:
                    emit_xload(k + 1)

                def fsq(s, k=k, xb=xb, n1=n1):
                    wait(s, t_xload[k])
                    wait(s, n1_free[k % 2])
                    wait(s, t_apart_dma[k - 1] if k > 0 else None)
                    i = s.activation(out=n1[:, :], in_=xb[:, :], func=AF.Square)
                    act.inc(i)
                block.scalar(fsq)
                t_sq = act.tok()

                def fstat(t, k=k, xb=xb, n1=n1, t_sq=t_sq):
                    wait(t, t_sq)
                    if k == 0:
                        wait(t, t_w)
                    pp.prod_wait(t, sps0)
                    mm = None
                    for dk in range(DK):
                        mm = t.matmul(sps0[:, :], ones_bf_col[:], xb[:, dk * TC:(dk + 1) * TC],
                                      start=(dk == 0), stop=(dk == DK - 1))
                    pe.inc(mm)
                    pp.prod_wait(t, sps1)
                    mm = None
                    for dk in range(DK):
                        mm = t.matmul(sps1[:, :], ones_bf_col[:], n1[:, dk * TC:(dk + 1) * TC],
                                      start=(dk == 0), stop=(dk == DK - 1))
                    pe.inc(mm)
                block.tensor(fstat)
                t_sums = pe.tok()

                t_rows_bf = emit_rows(t_sums, sps0, sps1)

                def fbc(t, t_rows_bf=t_rows_bf):
                    wait(t, t_rows_bf)
                    pp.prod_wait(t, bps)
                    i = t.matmul(bps[:, :], ones_bf_row[:, :], rowbf[0:1, 0:TC],
                                 start=True, stop=True)
                    pe.inc(i)
                block.tensor(fbc)
                t_b1 = pe.tok()

                def fbce(s, t_b1=t_b1):
                    wait(s, t_b1)
                    i = s.activation(out=bcastt[:, 0:TC], in_=bps[:, :], func=AF.Copy)
                    act.inc(i)
                block.scalar(fbce)
                t_b1e = act.tok()
                pp.consumed(bps, t_b1e)

                def fbc2(t, t_b1e=t_b1e):
                    pp.prod_wait(t, bps)
                    i = t.matmul(bps[:, :], ones_bf_row[:, :], rowbf[0:1, TC:2 * TC],
                                 start=True, stop=True)
                    pe.inc(i)
                block.tensor(fbc2)
                t_b2 = pe.tok()

                def fbc2e(s, t_b2=t_b2):
                    wait(s, t_b2)
                    i = s.activation(out=bcastt[:, TC:2 * TC], in_=bps[:, :], func=AF.Copy)
                    act.inc(i)
                block.scalar(fbc2e)
                t_bc = act.tok()
                pp.consumed(bps, t_bc)

                def fn1(v, xb=xb, n1=n1, k=k, t_bc=t_bc):
                    wait(v, t_bc)
                    wait(v, n1_free[k % 2])
                    i = None
                    for dk in range(DK):
                        i = v.tensor_tensor(out=n1[:, dk * TC:(dk + 1) * TC],
                                            in0=xb[:, dk * TC:(dk + 1) * TC],
                                            in1=bcastt[:, 0:TC], op=OP.mult)
                    dve.inc(i)
                block.vector(fn1)
                t_n1h = dve.tok()

                def fn1b(v, n1=n1, t_n1h=t_n1h):
                    wait(v, t_n1h)
                    last = None
                    for dk in range(DK):
                        last = v.tensor_tensor(out=n1[:, dk * TC:(dk + 1) * TC],
                                               in0=n1[:, dk * TC:(dk + 1) * TC],
                                               in1=bcastt[:, TC:2 * TC], op=OP.add)
                    dve.inc(last)
                block.vector(fn1b)
                t_n1r = dve.tok()
                xb_free[k % 2] = t_n1r

                if "n1" in taps and k == 0:
                    def tapn1(s, n1=n1, tk=t_n1r):
                        wait(s, tk)
                        i = s.dma_start(out=taps["n1"].ap()[:, :], in_=n1[:, :])
                        sdma.inc(i, 16)
                    block.sync(tapn1)

                def wmm(t, ps, ccx, n1=n1):
                    mm = None
                    for dk in range(DK):
                        mm = t.matmul(ps[:, :],
                                      win_sb[:, dk * 2 * CI + ccx * 128:dk * 2 * CI + (ccx + 1) * 128],
                                      n1[:, dk * TC:(dk + 1) * TC],
                                      start=(dk == 0), stop=(dk == DK - 1))
                    return mm

                def wev(s, ps, ccx):
                    if ccx < CC:
                        return s.activation(out=u_raw[:, ccx * (TC + 3) + 3:(ccx + 1) * (TC + 3)],
                                            in_=ps[:, :], func=AF.Copy)
                    cz = ccx - CC
                    return s.activation(out=szb[:, cz * TC:(cz + 1) * TC],
                                        in_=ps[:, :], func=AF.Silu)
                t_wevac = mm_ev(16, [wps0, wps1], wmm, wev,
                                head_pe=[t_n1r, t_halo[k],
                                         t_yg[k - 1] if k > 0 else None])
                n1_free[k % 2] = pe.tok()

                def fconv(g, k=k, t_wevac=t_wevac):
                    wait(g, t_wevac)
                    wait(g, t_yg[k - 1] if k > 0 else None)
                    wait(g, t_apart_dma[k - 1] if k > 0 else None)
                    for ccx in range(CC):
                        base = ccx * (TC + 3)
                        o = ucb[:, ccx * TC:(ccx + 1) * TC]
                        for tp in range(4):
                            w0 = bass.AP(convwc, ccx * 4 + tp, [[CC * 4, 128], [0, TC]])
                            i = g.tensor_tensor(out=cvt0[:, tp * TC:(tp + 1) * TC],
                                                in0=u_raw[:, base + tp:base + tp + TC],
                                                in1=w0, op=OP.mult)
                        pool.inc(i)
                        wait(g, pool.tok())
                        i = g.tensor_tensor(out=cvt0[:, 0:TC], in0=cvt0[:, 0:TC],
                                            in1=cvt0[:, TC:2 * TC], op=OP.add)
                        i = g.tensor_tensor(out=cvt0[:, 2 * TC:3 * TC],
                                            in0=cvt0[:, 2 * TC:3 * TC],
                                            in1=cvt0[:, 3 * TC:4 * TC], op=OP.add)
                        pool.inc(i)
                        wait(g, pool.tok())
                        i = g.tensor_tensor(out=cvt0[:, 0:TC], in0=cvt0[:, 0:TC],
                                            in1=cvt0[:, 2 * TC:3 * TC], op=OP.add)
                        pool.inc(i)
                        wait(g, pool.tok())
                        b0 = bass.AP(convbc, ccx, [[CC, 128], [0, TC]])
                        i = g.tensor_tensor(out=o, in0=cvt0[:, 0:TC], in1=b0, op=OP.add)
                        pool.inc(i)
                        wait(g, pool.tok())
                block.gpsimd(fconv)
                t_uc = pool.tok()

                def fhalo(g, tpr=pool.tok()):
                    wait(g, tpr)
                    last = None
                    for ccx in range(CC):
                        base = ccx * (TC + 3)
                        last = g.tensor_copy(out=u_raw[:, base + 0:base + 3],
                                             in_=u_raw[:, base + TC:base + TC + 3])
                    pool.inc(last)
                block.gpsimd(fhalo)
                t_halo[k + 1] = pool.tok()

                if "uc" in taps and k == 0:
                    def tapuc(s, tk=t_uc):
                        wait(s, tk)
                        i = s.dma_start(out=taps["uc"].ap()[:, :], in_=ucb[:, :])
                        sdma.inc(i, 16)
                    block.sync(tapuc)

                def fscan(v, k=k, t_uc=t_uc):
                    wait(v, t_uc)
                    if k == 0:
                        wait(v, t_pro)
                    for ccx in range(CC):
                        for half in range(2):
                            last = None
                            for i2 in range(8):
                                sx = half * 8 + i2
                                a0 = bass.AP(acols, ccx * NS + sx, [[128, 128], [0, TC]])
                                last = v.tensor_tensor_scan(
                                    out=scano[:, i2 * TC:(i2 + 1) * TC],
                                    data0=a0, data1=ucb[:, ccx * TC:(ccx + 1) * TC],
                                    initial=carry[:, ccx * NS + sx:ccx * NS + sx + 1],
                                    op0=OP.mult, op1=OP.add)
                            dve.inc(last)
                            wait(v, dve.tok())
                            v.tensor_copy(out=carry[:, ccx * NS + half * 8:ccx * NS + half * 8 + 8],
                                          in_=bass.AP(scano, TC - 1, [[8 * TC, 128], [TC, 8]]))
                            last = None
                            for i2 in range(4):
                                last = v.tensor_tensor(out=pairb[:, i2 * TC:(i2 + 1) * TC],
                                                       in0=scano[:, (2 * i2) * TC:(2 * i2 + 1) * TC],
                                                       in1=scano[:, (2 * i2 + 1) * TC:(2 * i2 + 2) * TC],
                                                       op=OP.add)
                            dve.inc(last)
                            wait(v, dve.tok())
                            v.tensor_tensor(out=quadb[:, 0:TC], in0=pairb[:, 0:TC],
                                            in1=pairb[:, TC:2 * TC], op=OP.add)
                            i3 = v.tensor_tensor(out=quadb[:, TC:2 * TC], in0=pairb[:, 2 * TC:3 * TC],
                                                 in1=pairb[:, 3 * TC:4 * TC], op=OP.add)
                            dve.inc(i3)
                            wait(v, dve.tok())
                            dsthalf = ybuf[:, ccx * TC:(ccx + 1) * TC] if half == 0                                 else quadb[:, 2 * TC:3 * TC]
                            i3 = v.tensor_tensor(out=dsthalf, in0=quadb[:, 0:TC],
                                                 in1=quadb[:, TC:2 * TC], op=OP.add)
                            dve.inc(i3)
                            wait(v, dve.tok())
                            if half == 1:
                                i3 = v.tensor_tensor(out=ybuf[:, ccx * TC:(ccx + 1) * TC],
                                                     in0=ybuf[:, ccx * TC:(ccx + 1) * TC],
                                                     in1=quadb[:, 2 * TC:3 * TC], op=OP.add)
                                dve.inc(i3)
                                wait(v, dve.tok())
                block.vector(fscan)
                t_ysum = dve.tok()

                def fyg(v, t_ysum=t_ysum):
                    wait(v, t_ysum)
                    i = None
                    for ccx in range(CC):
                        i = v.scalar_tensor_tensor(out=ybuf[:, ccx * TC:(ccx + 1) * TC],
                                                   in0=ucb[:, ccx * TC:(ccx + 1) * TC],
                                                   scalar=dpc[:, ccx:ccx + 1],
                                                   in1=ybuf[:, ccx * TC:(ccx + 1) * TC],
                                                   op0=OP.mult, op1=OP.add)
                    dve.inc(i)
                block.vector(fyg)
                t_y2 = dve.tok()

                def fyg2(v, t_y2=t_y2, k=k):
                    wait(v, t_y2)
                    wait(v, t_wevac)
                    last = None
                    for ccx in range(CC):
                        last = v.tensor_tensor(out=ygbuf[:, ccx * TC:(ccx + 1) * TC],
                                               in0=ybuf[:, ccx * TC:(ccx + 1) * TC],
                                               in1=szb[:, ccx * TC:(ccx + 1) * TC], op=OP.mult)
                    dve.inc(last)
                block.vector(fyg2)
                t_yg[k] = dve.tok()

                if "yg" in taps and k == 0:
                    def tapyg(s, tk=t_yg[k]):
                        wait(s, tk)
                        i = s.dma_start(out=taps["yg"].ap()[:, :], in_=ygbuf[:, :])
                        sdma.inc(i, 16)
                    block.sync(tapyg)
                if "y" in taps and k == 0:
                    def tapy(s, tk=t_y2):
                        wait(s, tk)
                        i = s.dma_start(out=taps["y"].ap()[:, :], in_=ybuf[:, :])
                        sdma.inc(i, 16)
                    block.sync(tapy)

                def omm(t, ps, dj):
                    mm = None
                    for ci in range(CC):
                        mm = t.matmul(ps[:, :],
                                      wout_sb[:, ci * D + dj * 128:ci * D + (dj + 1) * 128],
                                      ygbuf[:, ci * TC:(ci + 1) * TC],
                                      start=(ci == 0), stop=(ci == CC - 1))
                    return mm

                def oev(s, ps, dj):
                    return s.activation(out=apart[:, dj * TC:(dj + 1) * TC],
                                        in_=ps[:, :], func=AF.Copy)
                t_ap = mm_ev(DK, [ops0, ops1], omm, oev,
                             head_pe=[t_yg[k], t_apart_dma[k - 1] if k > 0 else None])

                if "apart" in taps and k == 0:
                    def tapap(s, tk=t_ap):
                        wait(s, tk)
                        i = s.dma_start(out=taps["apart"].ap()[:, :], in_=apart[:, :])
                        sdma.inc(i, 16)
                    block.sync(tapap)

                def fastore(s, k=k, t_ap=t_ap):
                    wait(s, t_ap)
                    i = s.dma_start(out=bass.AP(d_aloc, k * 128 * DKTC, [[DKTC, 128], [1, DKTC]]),
                                    in_=apart[:, :])
                    sdma.inc(i, 16)
                block.sync(fastore)
                t_apart_dma[k] = sdma.tok()

            t_attn_dma = sdma.tok()

        if cfg.get("stop") == "attn":
            def fend(s, tk=t_attn_dma):
                wait(s, tk)
            block.sync(fend)
            return nc

        def fxchg(g, t_attn_dma=t_attn_dma):
            wait(g, t_attn_dma)
            pid = g.partition_id()
            parity = g.snap(pid % 2, min_val=0, max_val=1)
            poff = g.snap(parity * (NCA * 128 * DKTC), min_val=0, max_val=NCA * 128 * DKTC)
            toff = g.snap(parity * (NCF * 128 * DKTC), min_val=0, max_val=NCF * 128 * DKTC)
            i = g.dma_start(out=bass.AP(shb, poff, [[128 * DKTC, NCA], [DKTC, 128], [1, DKTC]]),
                            in_=bass.AP(d_aloc, 0, [[128 * DKTC, NCA], [DKTC, 128], [1, DKTC]]))
            gdma.inc(i, 16)
            gdma.inc(g.dma_start(out=d_bin.ap()[:, :], in_=ones_f_row[0:1, 0:8]), 16)
            wait(g, gdma.tok())
            i = g.collective_compute("AllGather", OP.bypass,
                                     replica_groups=[[0, 1], [2, 3], [4, 5], [6, 7]],
                                     ins=[d_bin.ap().opt()],
                                     outs=[d_bout.ap().opt()])
            ccs.inc(i, 1)
            wait(g, ccs.tok())
            i = g.dma_start(out=bass.AP(d_ax, 0, [[128 * DKTC, NCF], [DKTC, 128], [1, DKTC]]),
                            in_=bass.AP(shb, toff, [[128 * DKTC, NCF], [DKTC, 128], [1, DKTC]]))
            gdma.inc(i, 16)
            i = g.dma_start(out=bass.AP(d_ax, NCF * 128 * DKTC,
                                        [[128 * DKTC, NCF], [DKTC, 128], [1, DKTC]]),
                            in_=bass.AP(shb, toff + NCA * 128 * DKTC,
                                        [[128 * DKTC, NCF], [DKTC, 128], [1, DKTC]]))
            gdma.inc(i, 16)
        block.gpsimd(fxchg)
        t_ax_ready = gdma.tok()

        if cfg.get("stop") == "xchg":
            def fend2(g, tk=t_ax_ready):
                wait(g, tk)
            block.gpsimd(fend2)
            return nc

        with ExitStack() as es_ffn:
            xn2_sb = es_ffn.enter_context(nc.sbuf_tensor("xn2sb", [128, DK * TC], BF16))
            ldbuf = es_ffn.enter_context(nc.sbuf_tensor("ldbuf", [128, DK * TC], F32))
            t32a = es_ffn.enter_context(nc.sbuf_tensor("t32a", [128, DK * TC], F32))
            t32b = es_ffn.enter_context(nc.sbuf_tensor("t32b", [128, DK * TC], F32))
            hbuf = es_ffn.enter_context(nc.sbuf_tensor("hbuf", [128, 32 * TC], BF16))
            wstr0 = es_ffn.enter_context(nc.sbuf_tensor("wstr0", [128, 4096], BF16))
            wstr1 = es_ffn.enter_context(nc.sbuf_tensor("wstr1", [128, 4096], BF16))
            abel_bf = es_ffn.enter_context(nc.sbuf_tensor("abel_bf", [128, DK * TC], BF16))
            nonab_bf = es_ffn.enter_context(nc.sbuf_tensor("nonab_bf", [128, DK * TC], BF16))
            accf = es_ffn.enter_context(nc.sbuf_tensor("accf", [128, DK * TC], F32))
            wT6 = es_ffn.enter_context(nc.sbuf_tensor("wT6", [NE, TC], F32))
            wbc = es_ffn.enter_context(nc.sbuf_tensor("wbc", [128, NE * TC], BF16))
            wscr = es_ffn.enter_context(nc.sbuf_tensor("wscr", [128, 96], F32))
            lgT = es_ffn.enter_context(nc.sbuf_tensor("lgT", [128, 32], F32))
            ttiny = es_ffn.enter_context(nc.sbuf_tensor("ttiny", [128, 32], F32))
            fbct = es_ffn.enter_context(nc.sbuf_tensor("fbct", [128, 3 * TC], F32))
            selrow = es_ffn.enter_context(nc.sbuf_tensor("selrow", [NE, NE * 128], F32))
            eb2r = es_ffn.enter_context(nc.sbuf_tensor("eb2r", [NE, D], F32))
            fps0 = es_ffn.enter_context(nc.psum_tensor("fps0", [128, TC], F32))
            fps1 = es_ffn.enter_context(nc.psum_tensor("fps1", [128, TC], F32))
            gps0 = es_ffn.enter_context(nc.psum_tensor("gps0", [128, TC], F32))
            gps1 = es_ffn.enter_context(nc.psum_tensor("gps1", [128, TC], F32))
            rps = es_ffn.enter_context(nc.psum_tensor("rps", [1, TC], F32))
            rps2 = es_ffn.enter_context(nc.psum_tensor("rps2", [1, TC], F32))
            lps = es_ffn.enter_context(nc.psum_tensor("lps", [NE, TC], F32))
            wtp = es_ffn.enter_context(nc.psum_tensor("wtp", [NE, TC], F32))

            def fsel(v, b=None):
                wait(v, (h_sdma, sdma.n))
                wait(v, (h_pool, pool.n))
                wait(v, (h_act, act.n))
                wait(v, (h_pe, pe.n))
                wait(v, (h_gdma, gdma.n))
                i = v.memset(selrow[:], 0.0)
                dve.inc(i)
            block.vector(fsel)
            t_sel0 = dve.tok()

            def gsel(g, t_sel0=t_sel0):
                wait(g, t_sel0)
                for e in range(NE):
                    init.inc(g.dma_start(out=selrow[e:e + 1, e * 128:(e + 1) * 128],
                                         in_=ones_f_row[0:1, :]), 16)
            block.gpsimd(gsel)
            t_sel = init.tok()

            def gbias(g):
                ebs.inc(g.dma_start(out=ab1row[:, :], in_=d_ab1.ap()[:, :]), 16)
                ebs.inc(g.dma_start(out=ab2row[:, :], in_=d_ab2.ap()[:, :]), 16)
                ebs.inc(g.dma_start(out=gatebrow[:, :], in_=d_gateb.ap()[:, :]), 16)
                ebs.inc(g.dma_start(out=gcbrow[:, :], in_=d_gcb.ap()[:, :]), 16)
            block.gpsimd(gbias)
            t_brow = ebs.tok()

            def leb2(s, b=None):
                wait(s, (h_pool, pool.n))
                wait(s, (h_dve, dve.n))
                wait(s, (h_act, act.n))
                wait(s, (h_pe, pe.n))
                i = s.dma_start(out=eb2r[:, :], in_=d_eb2.ap()[:, :])
                ldma.inc(i, 16)
            block.sync(leb2)
            t_eb2 = ldma.tok()

            def ln_stats(t_src_ready):
                def fsq2(s, tk=t_src_ready):
                    wait(s, tk)
                    i = s.activation(out=hbuf[:, 0:DK * TC], in_=hbuf[:, DK * TC:2 * DK * TC],
                                     func=AF.Square)
                    act.inc(i)
                block.scalar(fsq2)
                t_sq = act.tok()

                def fmm(t, t_sq=t_sq):
                    wait(t, t_sq)
                    pp.prod_wait(t, rps)
                    mm = None
                    for dk in range(DK):
                        mm = t.matmul(rps[:, :], ones_bf_col[:],
                                      hbuf[:, DK * TC + dk * TC:DK * TC + (dk + 1) * TC],
                                      start=(dk == 0), stop=(dk == DK - 1))
                    pe.inc(mm)
                    pp.prod_wait(t, rps2)
                    mm = None
                    for dk in range(DK):
                        mm = t.matmul(rps2[:, :], ones_bf_col[:], hbuf[:, dk * TC:(dk + 1) * TC],
                                      start=(dk == 0), stop=(dk == DK - 1))
                    pe.inc(mm)
                block.tensor(fmm)
                t_mm = pe.tok()

                tk = emit_rows(t_mm, rps, rps2)

                def fb1(t, tk=tk):
                    wait(t, tk)
                    pp.prod_wait(t, gps0)
                    i = t.matmul(gps0[:, :], ones_f_row[:, :], rows[0:1, TC:2 * TC],
                                 start=True, stop=True)
                    pe.inc(i)
                    pp.prod_wait(t, gps1)
                    i = t.matmul(gps1[:, :], ones_f_row[:, :], rows[0:1, 3 * TC:4 * TC],
                                 start=True, stop=True)
                    pe.inc(i)
                block.tensor(fb1)
                tk = pe.tok()

                def fb2(s, tk=tk):
                    wait(s, tk)
                    s.activation(out=fbct[:, 0:TC], in_=gps0[:, :], func=AF.Copy)
                    i = s.activation(out=fbct[:, TC:2 * TC], in_=gps1[:, :], func=AF.Copy)
                    act.inc(i)
                block.scalar(fb2)
                tk = act.tok()
                pp.consumed(gps0, tk)
                pp.consumed(gps1, tk)
                return tk

            def normalize(dst, src, tk_st, extra=None, t_extra=None):
                def fna(v, tk_st=tk_st, dst=dst, src=src, tsd=None):
                    wait(v, tk_st)
                    wait(v, (h_sdma, sdma.n))
                    i = None
                    for dk in range(DK):
                        i = v.tensor_tensor(out=dst[:, dk * TC:(dk + 1) * TC],
                                            in0=src[:, dk * TC:(dk + 1) * TC],
                                            in1=fbct[:, 0:TC], op=OP.mult)
                    dve.inc(i)
                block.vector(fna)
                tk = dve.tok()

                def fnb(v, tk=tk, dst=dst):
                    wait(v, tk)
                    i = None
                    for dk in range(DK):
                        i = v.tensor_tensor(out=dst[:, dk * TC:(dk + 1) * TC],
                                            in0=dst[:, dk * TC:(dk + 1) * TC],
                                            in1=fbct[:, TC:2 * TC], op=OP.add)
                    dve.inc(i)
                block.vector(fnb)
                tk = dve.tok()
                if extra is not None:
                    def fnc(v, tk=tk, t_extra=t_extra, dst=dst, extra=extra):
                        wait(v, tk)
                        wait(v, t_extra)
                        i = None
                        for dk in range(DK):
                            i = v.tensor_tensor(out=dst[:, dk * TC:(dk + 1) * TC],
                                                in0=dst[:, dk * TC:(dk + 1) * TC],
                                                in1=extra[:, dk * TC:(dk + 1) * TC], op=OP.add)
                        dve.inc(i)
                    block.vector(fnc)
                    tk = dve.tok()
                return tk

            def copy_sbf(src, tk_src):
                def f(v, tk_src=tk_src, src=src):
                    wait(v, tk_src)
                    i = v.tensor_copy(out=hbuf[:, DK * TC:2 * DK * TC], in_=src)
                    dve.inc(i)
                block.vector(f)
                return dve.tok()

            t_ld_free = None
            t_a_free = None
            t_prev_xn2dma = None
            for kf in range(NCF):
                def fload_a(g, kf=kf, tfree=t_ld_free, tfa=t_a_free):
                    wait(g, t_ax_ready)
                    wait(g, tfree)
                    wait(g, tfa)
                    gdma.inc(g.dma_start(out=ldbuf[:, 0:DK * TC],
                             in_=bass.AP(d_ax, kf * 128 * DKTC, [[DKTC, 128], [1, DKTC]])), 16)
                    gdma.inc(g.dma_start(out=t32a[:, :],
                             in_=bass.AP(d_ax, (NCF + kf) * 128 * DKTC,
                                         [[DKTC, 128], [1, DKTC]])), 16)
                block.gpsimd(fload_a)
                t_lda0 = gdma.tok()

                def fadda(v, tk=t_lda0):
                    wait(v, tk)
                    i = v.tensor_tensor(out=t32a[:, :], in0=t32a[:, :], in1=ldbuf[:, 0:DK * TC],
                                        op=OP.add)
                    dve.inc(i)
                block.vector(fadda)
                t_lda = dve.tok()

                def fload_x0(s, kf=kf, t_lda=t_lda):
                    wait(s, t_lda)
                    if kf == 0:
                        wait(s, (h_ldma, ldma.n))
                    i = s.dma_start(out=ldbuf[:, 0:DK * TC],
                                    in_=bass.AP(d_x0T, kf * 128 * DKTC, [[DKTC, 128], [1, DKTC]]))
                    ldma.inc(i, 16)
                block.sync(fload_x0)
                t_ldx0 = ldma.tok()

                a_sl = t32a[:, :]
                x0_sl = ldbuf[:, 0:DK * TC]

                t_abf = copy_sbf(a_sl, t_lda)
                tk_st = ln_stats(t_abf)
                t_x1 = normalize(t32b, a_sl, tk_st, extra=x0_sl, t_extra=t_ldx0)
                t_ld_free = t_x1

                if "x1c" in taps and kf == 0:
                    def tapx1(s, tk=t_x1):
                        wait(s, tk)
                        i = s.dma_start(out=taps["x1c"].ap()[:, :], in_=t32b[:, :])
                        sdma.inc(i, 16)
                    block.sync(tapx1)

                def fx1s(s, kf=kf, t_x1=t_x1):
                    wait(s, t_x1)
                    i = s.dma_start(out=bass.AP(d_x1, kf * 128 * DKTC, [[DKTC, 128], [1, DKTC]]),
                                    in_=t32b[:, :])
                    sdma.inc(i, 16)
                block.sync(fx1s)

                t_x1bf = copy_sbf(t32b[:, :], t_x1)
                tk_st2 = ln_stats(t_x1bf)
                t_n2 = normalize(t32a, t32b, tk_st2)

                def fn2s(s, kf=kf, t_n2=t_n2):
                    wait(s, t_n2)
                    i = s.dma_start(out=bass.AP(d_n2, kf * 128 * DKTC, [[DKTC, 128], [1, DKTC]]),
                                    in_=t32a[:, :])
                    sdma.inc(i, 16)
                block.sync(fn2s)

                t_n2bf = copy_sbf(t32a[:, :], t_n2)
                tk_st3 = ln_stats(t_n2bf)
                t_xpre = normalize(t32b, t32a, tk_st3)

                def fxn2c(v, t_xpre=t_xpre):
                    wait(v, t_xpre)
                    wait(v, (h_sdma, sdma.n))
                    last = None
                    for dk in range(DK):
                        last = v.tensor_scalar(out=xn2_sb[:, dk * TC:(dk + 1) * TC],
                                               in0=t32b[:, dk * TC:(dk + 1) * TC],
                                               scalar1=lngc[:, dk:dk + 1], scalar2=lnbc[:, dk:dk + 1],
                                               op0=OP.mult, op1=OP.add)
                    dve.inc(last)
                block.vector(fxn2c)
                t_xn2v = dve.tok()
                t_a_free = t_xn2v

                if "xn2" in taps and kf == 0:
                    def tapxn2(s, tk=t_xn2v):
                        wait(s, tk)
                        i = s.dma_start(out=taps["xn2"].ap()[:, :], in_=xn2_sb[:, :])
                        sdma.inc(i, 16)
                    block.sync(tapxn2)

                def fxn2s(s, kf=kf, tk=t_xn2v):
                    wait(s, tk)
                    i = s.dma_start(out=bass.AP(d_xn2, kf * 128 * DKTC, [[DKTC, 128], [1, DKTC]]),
                                    in_=xn2_sb[:, :])
                    sdma.inc(i, 16)
                block.sync(fxn2s)
                t_prev_xn2dma = sdma.tok()
            t_bound_done = t_prev_xn2dma
            if cfg.get("stop") == "bound":
                def fend3(s, tk=t_bound_done):
                    wait(s, tk)
                block.sync(fend3)
                return nc

            wbufs = [wstr0, wstr1]
            wb_free = [None, None]
            wslot = [0]

            def stream_w(dr_ap, ncols):
                sl = wslot[0] % 2
                wslot[0] += 1
                buf = wbufs[sl]
                sem = wst if sl == 0 else wst1

                def f(g, sl=sl, dr_ap=dr_ap, ncols=ncols, buf=buf, sem=sem):
                    wait(g, wb_free[sl])
                    i = g.dma_start(out=buf[:, 0:ncols], in_=dr_ap)
                    sem.inc(i, 16)
                block.gpsimd(f)
                return buf, sl, sem.tok()

            prev_st4 = None
            t_moe_prev = None
            prev_xn2_pe = None
            for kf in range(NCF):
                def fxl(g, kf=kf, tb=t_bound_done, tfree=prev_xn2_pe,
                        tdma=None):
                    wait(g, tb)
                    wait(g, tfree)
                    wait(g, (h_sdma, sdma.n))
                    if kf == 0:
                        wait(g, (h_gdma, gdma.n))
                    i = g.dma_start(out=xn2_sb[:, :],
                                    in_=bass.AP(d_xn2, kf * 128 * DKTC, [[DKTC, 128], [1, DKTC]]))
                    xns.inc(i, 16)
                block.gpsimd(fxl)
                t_xk = xns.tok()
                xk = xn2_sb

                for hb in range(8):
                    wtile, sl, t_wl = stream_w(d_aw1.ap()[hb * 128:(hb + 1) * 128, :], 4096)

                    def h1m(t, ps, sub, wtile=wtile, hb=hb):
                        hc = hb * 4 + sub
                        for dk in range(DK):
                            t.matmul(ps[:, :],
                                     wtile[:, dk * 512 + sub * 128:dk * 512 + (sub + 1) * 128],
                                     xk[:, dk * TC:(dk + 1) * TC],
                                     start=(dk == 0), stop=False)
                        return t.matmul(ps[:, :], ab1row[0:1, hc * 128:(hc + 1) * 128],
                                        ones_tc[:, :], start=False, stop=True)

                    def h1e(s, ps, sub, hb=hb):
                        hc = hb * 4 + sub
                        return s.activation(out=hbuf[:, hc * TC:(hc + 1) * TC], in_=ps[:, :],
                                            func=AF.Gelu)
                    mm_ev(4, [fps0, fps1], h1m, h1e,
                          head_pe=[t_wl, t_xk, t_brow if (hb == 0 and kf == 0) else None])
                    wb_free[sl] = pe.tok()
                t_h1 = act.tok()

                for dj in range(DK):
                    wtile, sl, t_wl = stream_w(d_aw2.ap()[dj * 128:(dj + 1) * 128, :], 4096)

                    def abm(t, ps, _i, wtile=wtile, dj=dj):
                        for kc in range(32):
                            t.matmul(ps[:, :], wtile[:, kc * 128:(kc + 1) * 128],
                                     hbuf[:, kc * TC:(kc + 1) * TC],
                                     start=(kc == 0), stop=False)
                        return t.matmul(ps[:, :], ab2row[0:1, dj * 128:(dj + 1) * 128],
                                        ones_tc[:, :], start=False, stop=True)

                    def abe(s, ps, _i, dj=dj):
                        return s.activation(out=abel_bf[:, dj * TC:(dj + 1) * TC], in_=ps[:, :],
                                            func=AF.Copy)
                    mm_ev(1, [fps0 if dj % 2 == 0 else fps1], abm, abe,
                          head_pe=[t_wl, t_h1],
                          head_ev=[prev_st4, (h_sdma, sdma.n)] if dj == 0 else None)
                    wb_free[sl] = pe.tok()
                t_abelbf = act.tok()

                if "abel" in taps and kf == 0:
                    def tapab(s, tk=t_abelbf):
                        wait(s, tk)
                        i = s.dma_start(out=taps["abel"].ap()[:, :], in_=abel_bf[:, :])
                        sdma.inc(i, 16)
                    block.sync(tapab)

                def flg(t, t_xk=t_xk):
                    wait(t, t_xk)
                    pp.prod_wait(t, lps)
                    for dk in range(DK):
                        t.matmul(lps[:, :], gatewc[:, dk * NE:(dk + 1) * NE],
                                 xk[:, dk * TC:(dk + 1) * TC],
                                 start=(dk == 0), stop=False)
                    mm = t.matmul(lps[:, :], gatebrow[0:1, :], ones_tc[:, :],
                                  start=False, stop=True)
                    pe.inc(mm)
                block.tensor(flg)
                t_lg = pe.tok()

                def flge(s, t_lg=t_lg):
                    wait(s, t_lg)
                    wait(s, (h_sdma, sdma.n))
                    i = s.activation(out=wT6[:, :], in_=lps[:, :], func=AF.Copy)
                    act.inc(i)
                block.scalar(flge)
                t_lgr = act.tok()
                pp.consumed(lps, t_lgr)

                def flgt(t, t_lgr=t_lgr):
                    wait(t, t_lgr)
                    pp.prod_wait(t, gps0)
                    mm = None
                    for j in range(4):
                        mm = t.transpose(bass.AP(gps0, j * NE, [[TC, 128], [1, NE]]),
                                         wT6[0:NE, j * 128:(j + 1) * 128], ident_f[0:NE, 0:NE])
                    pe.inc(mm)
                block.tensor(flgt)
                tk = pe.tok()

                def flgte(s, tk=tk):
                    wait(s, tk)
                    i = s.activation(out=lgT[:, 0:24], in_=bass.AP(gps0, 0, [[TC, 128], [1, 24]]),
                                     func=AF.Copy)
                    act.inc(i)
                block.scalar(flgte)
                t_lgT = act.tok()
                pp.consumed(gps0, t_lgT)

                def ftop1(v, t_lgT=t_lgT):
                    wait(v, t_lgT)
                    i = None
                    for j in range(4):
                        i = v.tensor_reduce(out=ttiny[:, j:j + 1], in_=lgT[:, j * 6:(j + 1) * 6],
                                            axis=mybir.AxisListType.X, op=OP.max)
                    dve.inc(i)
                block.vector(ftop1)
                tk = dve.tok()

                def ftop2(v, tk=tk):
                    wait(v, tk)
                    i = None
                    for j in range(4):
                        i = v.tensor_scalar(out=wscr[:, j * 6:(j + 1) * 6],
                                            in0=lgT[:, j * 6:(j + 1) * 6],
                                            scalar1=ttiny[:, j:j + 1], scalar2=None,
                                            op0=OP.is_equal)
                    dve.inc(i)
                block.vector(ftop2)
                tk = dve.tok()

                def ftop3(v, tk=tk):
                    wait(v, tk)
                    i = None
                    for j in range(4):
                        i = v.scalar_tensor_tensor(out=wscr[:, 24 + j * 6:24 + (j + 1) * 6],
                                                   in0=wscr[:, j * 6:(j + 1) * 6], scalar=-1e30,
                                                   in1=lgT[:, j * 6:(j + 1) * 6],
                                                   op0=OP.mult, op1=OP.add)
                    dve.inc(i)
                block.vector(ftop3)
                tk = dve.tok()

                def ftop4(v, tk=tk):
                    wait(v, tk)
                    i = None
                    for j in range(4):
                        i = v.tensor_reduce(out=ttiny[:, 4 + j:5 + j],
                                            in_=wscr[:, 24 + j * 6:24 + (j + 1) * 6],
                                            axis=mybir.AxisListType.X, op=OP.max)
                    dve.inc(i)
                block.vector(ftop4)
                tk = dve.tok()

                def ftop5(v, tk=tk):
                    wait(v, tk)
                    i = v.tensor_tensor(out=ttiny[:, 8:12], in0=ttiny[:, 4:8], in1=ttiny[:, 0:4],
                                        op=OP.subtract)
                    dve.inc(i)
                block.vector(ftop5)
                tk = dve.tok()

                def ftop6(s, tk=tk):
                    wait(s, tk)
                    i = s.activation(out=ttiny[:, 12:16], in_=ttiny[:, 8:12], func=AF.Exp)
                    act.inc(i)
                block.scalar(ftop6)
                tk = act.tok()

                def ftop7(v, tk=tk):
                    wait(v, tk)
                    i = v.tensor_scalar(out=ttiny[:, 16:20], in0=ttiny[:, 12:16], scalar1=1.0,
                                        scalar2=None, op0=OP.add)
                    dve.inc(i)
                block.vector(ftop7)
                tk = dve.tok()

                def ftop8(v, tk=tk):
                    wait(v, tk)
                    i = v.reciprocal(out=ttiny[:, 20:24], in_=ttiny[:, 16:20])
                    dve.inc(i)
                block.vector(ftop8)
                tk = dve.tok()

                def ftop9(v, tk=tk):
                    wait(v, tk)
                    i = v.tensor_tensor(out=ttiny[:, 24:28], in0=ttiny[:, 12:16],
                                        in1=ttiny[:, 20:24], op=OP.mult)
                    dve.inc(i)
                block.vector(ftop9)
                tk = dve.tok()

                def ftop10(v, tk=tk):
                    wait(v, tk)
                    i = None
                    for j in range(4):
                        i = v.tensor_scalar(out=wscr[:, 48 + j * 6:48 + (j + 1) * 6],
                                            in0=wscr[:, 24 + j * 6:24 + (j + 1) * 6],
                                            scalar1=ttiny[:, 4 + j:5 + j], scalar2=None,
                                            op0=OP.is_equal)
                    dve.inc(i)
                block.vector(ftop10)
                tk = dve.tok()

                def ftop11(v, tk=tk):
                    wait(v, tk)
                    i = None
                    for j in range(4):
                        i = v.tensor_scalar(out=wscr[:, 48 + j * 6:48 + (j + 1) * 6],
                                            in0=wscr[:, 48 + j * 6:48 + (j + 1) * 6],
                                            scalar1=ttiny[:, 24 + j:25 + j], scalar2=None,
                                            op0=OP.mult)
                    dve.inc(i)
                block.vector(ftop11)
                tk = dve.tok()

                def ftop12(v, tk=tk):
                    wait(v, tk)
                    last = None
                    for j in range(4):
                        last = v.scalar_tensor_tensor(out=wscr[:, 72 + j * 6:72 + (j + 1) * 6],
                                                      in0=wscr[:, j * 6:(j + 1) * 6],
                                                      scalar=ttiny[:, 20 + j:21 + j],
                                                      in1=wscr[:, 48 + j * 6:48 + (j + 1) * 6],
                                                      op0=OP.mult, op1=OP.add)
                    dve.inc(last)
                block.vector(ftop12)
                t_wfull = dve.tok()

                def fwt(t, t_wfull=t_wfull):
                    wait(t, t_wfull)
                    pp.prod_wait(t, wtp)
                    mm = None
                    for j in range(4):
                        mm = t.transpose(wtp[0:NE, j * 128:(j + 1) * 128],
                                         wscr[:, 72 + j * 6:72 + (j + 1) * 6], ident_f[:, :])
                    pe.inc(mm)
                block.tensor(fwt)
                tk = pe.tok()

                def fwte(s, tk=tk):
                    wait(s, tk)
                    i = s.activation(out=wT6[:, :], in_=wtp[0:NE, :], func=AF.Copy)
                    act.inc(i)
                block.scalar(fwte)
                t_wT6 = act.tok()
                pp.consumed(wtp, t_wT6)

                if "wT6" in taps and kf == 0:
                    def tapw(s, tk=t_wT6):
                        wait(s, tk)
                        i = s.dma_start(out=taps["wT6"].ap()[0:NE, :], in_=wT6[:, :])
                        sdma.inc(i, 16)
                    block.sync(tapw)

                for e in range(NE):
                    def fwb(t, e=e, t_wT6=t_wT6):
                        wait(t, t_wT6)
                        if e == 0:
                            wait(t, t_sel)
                        ps = gps0 if e % 2 == 0 else gps1
                        pp.prod_wait(t, ps)
                        i = t.matmul(ps[:, :], selrow[0:NE, e * 128:(e + 1) * 128], wT6[:, :],
                                     start=True, stop=True)
                        pe.inc(i)
                        g_toks1[32 + e] = pe.tok()
                    block.tensor(fwb)

                    def fwbe(s, e=e):
                        wait(s, g_toks1[32 + e])
                        ps = gps0 if e % 2 == 0 else gps1
                        i = s.activation(out=wbc[:, e * TC:(e + 1) * TC], in_=ps[:, :], func=AF.Copy)
                        act.inc(i)
                        pp.consumed(ps, act.tok())
                    block.scalar(fwbe)
                t_wbc = act.tok()

                def a0m(t, ps, dj):
                    return t.matmul(ps[:, :], eb2r[0:NE, dj * 128:(dj + 1) * 128], wT6[:, :],
                                    start=True, stop=True)

                def a0e(s, ps, dj):
                    return s.activation(out=accf[:, dj * TC:(dj + 1) * TC], in_=ps[:, :],
                                        func=AF.Copy)
                t_acc = mm_ev(DK, [fps0, fps1], a0m, a0e,
                              head_pe=[t_wT6, t_eb2],
                              head_ev=[t_moe_prev, (h_sdma, sdma.n)])

                for e in range(NE):
                    def febrow(g, e=e, tprev=(h_pe, pe.n)):
                        wait(g, tprev)
                        ebs.inc(g.dma_start(out=eb1row[:, :],
                                            in_=d_eb1.ap()[0:1, e * HM:(e + 1) * HM]), 16)
                    block.gpsimd(febrow)
                    t_ebrow = ebs.tok()
                    for hb in range(4):
                        wtile, sl, t_wl = stream_w(
                            d_ew1.ap()[e, hb * 128:(hb + 1) * 128, :], 4096)

                        def hem(t, ps, sub, wtile=wtile, hb=hb):
                            hc = hb * 4 + sub
                            for dk in range(DK):
                                t.matmul(ps[:, :],
                                         wtile[:, dk * 512 + sub * 128:dk * 512 + (sub + 1) * 128],
                                         xk[:, dk * TC:(dk + 1) * TC],
                                         start=(dk == 0), stop=False)
                            return t.matmul(ps[:, :], eb1row[0:1, hc * 128:(hc + 1) * 128],
                                            ones_tc[:, :], start=False, stop=True)

                        def hee(s, ps, sub, hb=hb, e=e):
                            hc = hb * 4 + sub
                            return s.activation(
                                out=hbuf[:, hc * TC:(hc + 1) * TC], in_=ps[:, :],
                                func=AF.Gelu)
                        mm_ev(4, [fps0, fps1], hem, hee, head_pe=[t_wl, t_xk, t_ebrow])
                        wb_free[sl] = pe.tok()
                    t_he = act.tok()

                    for dj in range(DK):
                        wtile, sl, t_wl = stream_w(
                            d_ew2.ap()[e, dj * 128:(dj + 1) * 128, :], 2048)

                        def feo(t, dj=dj, wtile=wtile, t_wl=t_wl, t_he=t_he):
                            wait(t, t_wl)
                            wait(t, t_he)
                            ps = fps0 if dj % 2 == 0 else fps1
                            pp.prod_wait(t, ps)
                            mm = None
                            for kc in range(16):
                                mm = t.matmul(ps[:, :], wtile[:, kc * 128:(kc + 1) * 128],
                                              hbuf[:, kc * TC:(kc + 1) * TC],
                                              start=(kc == 0), stop=(kc == 15))
                            pe.inc(mm)
                            g_toks2[16 + dj] = pe.tok()
                        block.tensor(feo)
                        wb_free[sl] = pe.tok()

                        def feos(v, dj=dj, e=e, t_wbc=t_wbc):
                            wait(v, g_toks2[16 + dj])
                            wait(v, t_wbc)
                            ps = fps0 if dj % 2 == 0 else fps1
                            i = v.tensor_tensor(out=t32b[:, dj * TC:(dj + 1) * TC], in0=ps[:, :],
                                                in1=wbc[:, e * TC:(e + 1) * TC], op=OP.mult)
                            dve.inc(i)
                            pp.consumed(ps, dve.tok())
                            g_toks2[32 + dj] = dve.tok()
                        block.vector(feos)

                        def feoa(g, dj=dj, e=e, t_acc=t_acc):
                            wait(g, g_toks2[32 + dj])
                            if e == 0:
                                wait(g, t_acc)
                            i = g.tensor_tensor(out=accf[:, dj * TC:(dj + 1) * TC],
                                                in0=accf[:, dj * TC:(dj + 1) * TC],
                                                in1=t32b[:, dj * TC:(dj + 1) * TC], op=OP.add)
                            pool.inc(i)
                        block.gpsimd(feoa)
                t_moe = pool.tok()

                if "acc" in taps and kf == 0:
                    def tapacc(s, tk=t_moe):
                        wait(s, tk)
                        i = s.dma_start(out=taps["acc"].ap()[:, :], in_=accf[:, :])
                        sdma.inc(i, 16)
                    block.sync(tapacc)

                def fmc(v, t_moe=t_moe):
                    wait(v, t_moe)
                    wait(v, (h_sdma, sdma.n))
                    i = v.tensor_copy(out=nonab_bf[:, :], in_=accf[:, :])
                    dve.inc(i)
                block.vector(fmc)
                t_accbf = dve.tok()
                t_moe_prev = t_accbf

                def rtm(t, ps, dj):
                    mm = None
                    for kc in range(DK):
                        mm = t.matmul(ps[:, :],
                                      Rmat[:, kc * D + dj * 128:kc * D + (dj + 1) * 128],
                                      nonab_bf[:, kc * TC:(kc + 1) * TC],
                                      start=(kc == 0), stop=(kc == DK - 1))
                    return mm

                def rte(s, ps, dj):
                    return s.activation(out=t32a[:, dj * TC:(dj + 1) * TC], in_=ps[:, :],
                                        func=AF.Copy)
                t_rot = mm_ev(DK, [fps0, fps1], rtm, rte, head_pe=[t_accbf, t_R])

                def frotbf(v, t_rot=t_rot):
                    wait(v, t_rot)
                    i = v.tensor_copy(out=nonab_bf[:, :], in_=t32a[:, :])
                    dve.inc(i)
                block.vector(frotbf)
                t_nonbf = dve.tok()

                if "nonab" in taps and kf == 0:
                    def tapnb(s, tk=t_nonbf):
                        wait(s, tk)
                        i = s.dma_start(out=taps["nonab"].ap()[:, :], in_=nonab_bf[:, :])
                        sdma.inc(i, 16)
                    block.sync(tapnb)

                def fgc(t, t_abelbf=t_abelbf, t_nonbf=t_nonbf):
                    wait(t, t_abelbf)
                    wait(t, t_nonbf)
                    pp.prod_wait(t, rps)
                    mm = None
                    for dk in range(DK):
                        mm = t.matmul(rps[:, :], gcwc[:, dk:dk + 1],
                                      abel_bf[:, dk * TC:(dk + 1) * TC],
                                      start=(dk == 0), stop=False)
                    for dk in range(DK):
                        mm = t.matmul(rps[:, :], gcwc[:, DK + dk:DK + dk + 1],
                                      nonab_bf[:, dk * TC:(dk + 1) * TC],
                                      start=False, stop=False)
                    mm = t.matmul(rps[:, :], gcbrow[0:1, :], ones_tc[:, :],
                                  start=False, stop=True)
                    pe.inc(mm)
                block.tensor(fgc)
                t_gc = pe.tok()

                def fgce(s, t_gc=t_gc):
                    wait(s, t_gc)
                    i = s.activation(out=rows[0:1, 4 * TC:5 * TC], in_=rps[:, :], func=AF.Sigmoid)
                    act.inc(i)
                block.scalar(fgce)
                t_sig = act.tok()
                pp.consumed(rps, t_sig)

                def fgb(t, t_sig=t_sig):
                    wait(t, t_sig)
                    pp.prod_wait(t, gps0)
                    i = t.matmul(gps0[:, :], ones_f_row[:, :], rows[0:1, 4 * TC:5 * TC],
                                 start=True, stop=True)
                    pe.inc(i)
                block.tensor(fgb)
                tk = pe.tok()

                def fgbe(s, tk=tk):
                    wait(s, tk)
                    i = s.activation(out=fbct[:, 2 * TC:3 * TC], in_=gps0[:, :], func=AF.Copy)
                    act.inc(i)
                block.scalar(fgbe)
                t_gb = act.tok()
                pp.consumed(gps0, t_gb)

                def fmx1(v, t_gb=t_gb):
                    wait(v, t_gb)
                    wait(v, (h_sdma, sdma.n))
                    i = None
                    for dk in range(DK):
                        i = v.tensor_tensor(out=t32b[:, dk * TC:(dk + 1) * TC],
                                            in0=nonab_bf[:, dk * TC:(dk + 1) * TC],
                                            in1=abel_bf[:, dk * TC:(dk + 1) * TC], op=OP.subtract)
                    dve.inc(i)
                block.vector(fmx1)
                tk = dve.tok()

                def fmx2(v, tk=tk):
                    wait(v, tk)
                    i = None
                    for dk in range(DK):
                        i = v.tensor_tensor(out=t32b[:, dk * TC:(dk + 1) * TC],
                                            in0=t32b[:, dk * TC:(dk + 1) * TC],
                                            in1=fbct[:, 2 * TC:3 * TC], op=OP.mult)
                    dve.inc(i)
                block.vector(fmx2)
                tk = dve.tok()

                def fmx3(v, tk=tk):
                    wait(v, tk)
                    last = None
                    for dk in range(DK):
                        last = v.tensor_tensor(out=t32b[:, dk * TC:(dk + 1) * TC],
                                               in0=t32b[:, dk * TC:(dk + 1) * TC],
                                               in1=abel_bf[:, dk * TC:(dk + 1) * TC], op=OP.add)
                    dve.inc(last)
                block.vector(fmx3)
                t_mix = dve.tok()

                def fn2l(s, kf=kf, tdve=(h_dve, dve.n)):
                    wait(s, tdve)
                    i = s.dma_start(out=ldbuf[:, 0:DK * TC],
                                    in_=bass.AP(d_n2, kf * 128 * DKTC, [[DKTC, 128], [1, DKTC]]))
                    ldma.inc(i, 16)
                block.sync(fn2l)
                t_n2l = ldma.tok()

                def ff(v, t_mix=t_mix, t_n2l=t_n2l):
                    wait(v, t_mix)
                    wait(v, t_n2l)
                    last = None
                    for dk in range(DK):
                        last = v.tensor_tensor(out=t32b[:, dk * TC:(dk + 1) * TC],
                                               in0=t32b[:, dk * TC:(dk + 1) * TC],
                                               in1=ldbuf[:, dk * TC:(dk + 1) * TC], op=OP.add)
                    dve.inc(last)
                block.vector(ff)
                t_f = dve.tok()

                if "f" in taps and kf == 0:
                    def tapf(s, tk=t_f):
                        wait(s, tk)
                        i = s.dma_start(out=taps["f"].ap()[:, :], in_=t32b[:, :])
                        sdma.inc(i, 16)
                    block.sync(tapf)

                t_fbf = copy_sbf(t32b[:, :], t_f)
                tk_st4 = ln_stats(t_fbf)
                prev_st4 = tk_st4

                def fx1l(s, kf=kf, tdve=(h_dve, dve.n)):
                    wait(s, tdve)
                    i = s.dma_start(out=ldbuf[:, 0:DK * TC],
                                    in_=bass.AP(d_x1, kf * 128 * DKTC, [[DKTC, 128], [1, DKTC]]))
                    ldma.inc(i, 16)
                block.sync(fx1l)
                t_x1l = ldma.tok()

                t_x2 = normalize(t32b, t32b, tk_st4,
                                 extra=ldbuf[:, 0:DK * TC], t_extra=t_x1l)

                def fst(s, kf=kf, t_x2=t_x2):
                    wait(s, t_x2)
                    i = s.dma_start(out=bass.AP(d_out, kf * 128 * DKTC, [[DKTC, 128], [1, DKTC]]),
                                    in_=t32b[:, :])
                    sdma.inc(i, 16)
                block.sync(fst)
                prev_xn2_pe = pe.tok()

            t_final = sdma.tok()

            def ffin(s, t_final=t_final):
                wait(s, t_final)
            block.sync(ffin)

    return nc


def _tmaj(v, nch):
    T_, Dd = v.shape
    return np.ascontiguousarray(
        v.reshape(nch, TC, Dd // 128, 128).transpose(0, 3, 2, 1).reshape(nch * 128, -1))


def _wmaj(w, bs):
    K_, M_ = w.shape
    return np.ascontiguousarray(
        w.reshape(K_ // 128, 128, M_ // bs, bs).transpose(2, 1, 0, 3)
        .reshape((M_ // bs) * 128, (K_ // 128) * bs))


def _rmaj(w):
    K_, M_ = w.shape
    return np.ascontiguousarray(
        w.reshape(K_ // 128, 128, M_).transpose(1, 0, 2).reshape(128, -1))


def kernel(**inputs):
    cfg = CFG
    x = np.asarray(inputs["L0_attn_x"])
    W_in = np.asarray(inputs["L0_attn_W_in"])
    conv_w = np.asarray(inputs["L0_attn_conv_w"])
    conv_b = np.asarray(inputs["L0_attn_conv_b"])
    A_log = np.asarray(inputs["L0_attn_A_log"])
    D_param = np.asarray(inputs["L0_attn_D_param"])
    W_out = np.asarray(inputs["L0_attn_W_out"])

    nc = build_program(cfg)
    NCA, NCF = cfg["NCA"], cfg["NCF"]
    aw1_t = _wmaj(np.asarray(inputs["L0_ffn_aw1"]), 512)
    aw2_t = _wmaj(np.asarray(inputs["L0_ffn_aw2"]), 128)
    ew1_t = np.ascontiguousarray(np.stack(
        [_wmaj(np.asarray(inputs["L0_ffn_ew1"])[e], 512) for e in range(NE)]))
    ew2_t = np.ascontiguousarray(np.stack(
        [_wmaj(np.asarray(inputs["L0_ffn_ew2"])[e], 128) for e in range(NE)]))
    gatew_t = _rmaj(np.asarray(inputs["L0_ffn_gate_w"]))
    in_maps = []
    for c in range(8):
        b, j = c // 2, c % 2
        m = {
            "xT": _tmaj(x[b], NCA),
            "x0T": _tmaj(x[b][j * 4096:(j + 1) * 4096], NCF),
            "win": _rmaj(np.concatenate(
                [W_in[:, j * 1024:(j + 1) * 1024],
                 W_in[:, 2048 + j * 1024:2048 + (j + 1) * 1024]], axis=1)),
            "wout": _rmaj(W_out[j * 1024:(j + 1) * 1024, :]),
            "convw": _rmaj(conv_w[j * 1024:(j + 1) * 1024, 0, :]),
            "convb": np.ascontiguousarray(conv_b[None, j * 1024:(j + 1) * 1024]),
            "alog": np.ascontiguousarray(A_log[:, j * 1024:(j + 1) * 1024]),
            "dp": np.ascontiguousarray(D_param[None, j * 1024:(j + 1) * 1024]),
            "lng": np.ascontiguousarray(np.asarray(inputs["L0_ffn_ln_g"])[None, :]),
            "lnb": np.ascontiguousarray(np.asarray(inputs["L0_ffn_ln_b"])[None, :]),
            "aw1": aw1_t,
            "ab1": np.ascontiguousarray(np.asarray(inputs["L0_ffn_ab1"])[None, :]),
            "aw2": aw2_t,
            "ab2": np.ascontiguousarray(np.asarray(inputs["L0_ffn_ab2"])[None, :]),
            "gatew": gatew_t,
            "gateb": np.ascontiguousarray(np.asarray(inputs["L0_ffn_gate_b"])[None, :]),
            "ew1": ew1_t,
            "eb1": np.ascontiguousarray(np.asarray(inputs["L0_ffn_eb1"]).reshape(1, -1)),
            "ew2": ew2_t,
            "eb2": np.asarray(inputs["L0_ffn_eb2"]),
            "gens": _rmaj(np.asarray(inputs["L0_ffn_gens"])[c % 4]),
            "gcw": np.ascontiguousarray(np.asarray(inputs["L0_ffn_gc_w"]).reshape(1, -1)),
            "gcb": np.ascontiguousarray(np.asarray(inputs["L0_ffn_gc_b"]).reshape(1, 1)),
        }
        in_maps.append(m)

    res = run_bass_kernel_spmd(nc, in_maps, core_ids=list(range(8)))
    out = np.empty((4, 8192, 1024), np.float32)
    for c in range(8):
        b, j = c // 2, c % 2
        r = np.asarray(res.results[c]["x2T"])
        out[b, j * 4096:(j + 1) * 4096, :] = (
            r.reshape(NCF, 128, DK, TC).transpose(0, 3, 2, 1).reshape(NCF * TC, D))
    return out


# version 04
# speedup vs baseline: 1.0554x; 1.0554x over previous
import numpy as np
import ml_dtypes
import concourse.bass as bass
import concourse.mybir as mybir
from contextlib import ExitStack
from concourse.bass_utils import run_bass_kernel_spmd

F32 = mybir.dt.float32
BF16 = mybir.dt.bfloat16
AF = mybir.ActivationFunctionType
OP = mybir.AluOpType
NPBF = ml_dtypes.bfloat16

D = 1024
CI = 1024
NS = 16
TC = 512
DK = 8
CC = 8
H4 = 4096
HM = 2048
NE = 6
EPS = 1e-5
EXPM_DEG = 14

DKTC = DK * TC
CFG = dict(NCA=16, NCF=8, taps=())

g_toks1 = [None] * 64
g_toks2 = [None] * 64


class Sem:
    def __init__(self, h):
        self.h = h
        self.n = 0

    def inc(self, instr, k=1):
        instr.then_inc(self.h, k)
        self.n += k

    def tok(self):
        return (self.h, self.n)


def wait(eng, tok):
    if tok is not None and tok[1] > 0:
        eng.wait_ge(tok[0], tok[1])


class PsumPool:
    def __init__(self):
        self.cons = {}

    def prod_wait(self, eng, tile):
        wait(eng, self.cons.get(id(tile)))

    def consumed(self, tile, tok):
        self.cons[id(tile)] = tok


def build_program(cfg):
    NCA = cfg["NCA"]
    NCF = cfg["NCF"]
    L = NCA * TC
    TOK = NCF * TC

    nc = bass.Bass("TRN2", target_bir_lowering=False)

    d_xT = nc.dram_tensor("xT", [NCA * 128, DKTC], F32, kind="ExternalInput")
    d_x0T = nc.dram_tensor("x0T", [NCF * 128, DKTC], F32, kind="ExternalInput")
    d_win = nc.dram_tensor("win", [128, DK * 2 * CI], F32, kind="ExternalInput")
    d_wout = nc.dram_tensor("wout", [128, CC * D], F32, kind="ExternalInput")
    d_convw = nc.dram_tensor("convw", [128, CC * 4], F32, kind="ExternalInput")
    d_convb = nc.dram_tensor("convb", [1, CI], F32, kind="ExternalInput")
    d_alog = nc.dram_tensor("alog", [NS, CI], F32, kind="ExternalInput")
    d_dp = nc.dram_tensor("dp", [1, CI], F32, kind="ExternalInput")
    d_lng = nc.dram_tensor("lng", [1, D], F32, kind="ExternalInput")
    d_lnb = nc.dram_tensor("lnb", [1, D], F32, kind="ExternalInput")
    d_aw1 = nc.dram_tensor("aw1", [8 * 128, 4096], F32, kind="ExternalInput")
    d_ab1 = nc.dram_tensor("ab1", [1, H4], F32, kind="ExternalInput")
    d_aw2 = nc.dram_tensor("aw2", [8 * 128, 4096], F32, kind="ExternalInput")
    d_ab2 = nc.dram_tensor("ab2", [1, D], F32, kind="ExternalInput")
    d_gatew = nc.dram_tensor("gatew", [128, DK * NE], F32, kind="ExternalInput")
    d_gateb = nc.dram_tensor("gateb", [1, NE], F32, kind="ExternalInput")
    d_ew1 = nc.dram_tensor("ew1", [NE, 4 * 128, 4096], F32, kind="ExternalInput")
    d_eb1 = nc.dram_tensor("eb1", [1, NE * HM], F32, kind="ExternalInput")
    d_ew2 = nc.dram_tensor("ew2", [NE, 8 * 128, 2048], F32, kind="ExternalInput")
    d_eb2 = nc.dram_tensor("eb2", [NE, D], F32, kind="ExternalInput")
    d_gens = nc.dram_tensor("gens", [128, DK * D], F32, kind="ExternalInput")
    d_gcw = nc.dram_tensor("gcw", [1, 2 * D], F32, kind="ExternalInput")
    d_gcb = nc.dram_tensor("gcb", [1, 1], F32, kind="ExternalInput")

    d_out = nc.dram_tensor("x2T", [NCF * 128, DKTC], F32, kind="ExternalOutput")

    d_aloc = nc.dram_tensor("aloc", [NCA * 128, DKTC], F32)
    d_ax = nc.dram_tensor("ax", [2 * NCF * 128, DKTC], F32)
    d_x1 = nc.dram_tensor("x1loc", [NCF * 128, DKTC], F32)
    d_n2 = nc.dram_tensor("n2loc", [NCF * 128, DKTC], F32)
    shb = nc.dram_tensor("shb", [2 * NCA * 128 * DKTC], F32, addr_space="Shared")
    d_xn2 = nc.dram_tensor("xn2loc", [NCF * 128, DKTC], BF16)
    s_aw1 = nc.dram_tensor("s_aw1", [8 * 128, 4096], BF16)
    s_aw2 = nc.dram_tensor("s_aw2", [8 * 128, 4096], BF16)
    s_ew1 = nc.dram_tensor("s_ew1", [NE, 4 * 128, 4096], BF16)
    s_ew2 = nc.dram_tensor("s_ew2", [NE, 8 * 128, 2048], BF16)
    s_eb1 = nc.dram_tensor("s_eb1", [1, NE * HM], BF16)
    d_ein = nc.dram_tensor("ein", [D, D], BF16)
    d_eout = nc.dram_tensor("eout", [8 * D, D], BF16)
    d_bin = nc.dram_tensor("barrin", [1, 8], F32)
    d_bout = nc.dram_tensor("barrout", [2, 8], F32)

    taps = {}
    for t in cfg.get("taps", ()):
        shp, dt = {
            "n1": ([128, DK * TC], BF16), "uc": ([128, CC * TC], BF16),
            "y": ([128, CC * TC], BF16), "yg": ([128, CC * TC], BF16),
            "R": ([128, DK * D], BF16), "ET": ([128, DK * D], BF16),
            "xn2": ([128, DK * TC], BF16), "abel": ([128, DK * TC], BF16),
            "acc": ([128, DK * TC], F32), "nonab": ([128, DK * TC], BF16),
            "wT6": ([NS, TC], BF16), "x1c": ([128, DK * TC], F32),
            "apart": ([128, DK * TC], F32), "f": ([128, DK * TC], F32),
        }[t]
        taps[t] = nc.dram_tensor("dbg_" + t, shp, dt, kind="ExternalOutput")

    with ExitStack() as es_top:
        ones_bf_col = es_top.enter_context(nc.sbuf_tensor("ones_bf_col", [128, 1], BF16))
        ones_f_row = es_top.enter_context(nc.sbuf_tensor("ones_f_row", [1, 128], F32))
        ones_bf_row = es_top.enter_context(nc.sbuf_tensor("ones_bf_row", [1, 128], BF16))
        one11 = es_top.enter_context(nc.sbuf_tensor("one11", [1, 1], F32))
        ident_bf = es_top.enter_context(nc.sbuf_tensor("ident_bf", [128, 128], BF16))
        ident_f = es_top.enter_context(nc.sbuf_tensor("ident_f", [128, 128], F32))
        acols = es_top.enter_context(nc.sbuf_tensor("acols", [128, 128], F32))
        convwc = es_top.enter_context(nc.sbuf_tensor("convwc", [128, CC * 4], F32))
        convbc = es_top.enter_context(nc.sbuf_tensor("convbc", [128, CC], F32))
        dpc = es_top.enter_context(nc.sbuf_tensor("dpc", [128, CC], F32))
        lngc = es_top.enter_context(nc.sbuf_tensor("lngc", [128, DK], F32))
        lnbc = es_top.enter_context(nc.sbuf_tensor("lnbc", [128, DK], F32))
        gatewc = es_top.enter_context(nc.sbuf_tensor("gatewc", [128, DK * NE], BF16))
        gcwc = es_top.enter_context(nc.sbuf_tensor("gcwc", [128, 16], BF16))
        gcbc = es_top.enter_context(nc.sbuf_tensor("gcbc", [1, 1], F32))
        Rmat = es_top.enter_context(nc.sbuf_tensor("Rmat", [128, DK * D], BF16))
        carry = es_top.enter_context(nc.sbuf_tensor("carry", [128, 128], F32))
        rows = es_top.enter_context(nc.sbuf_tensor("rows", [1, 4 * TC], F32))
        rowbf = es_top.enter_context(nc.sbuf_tensor("rowbf", [1, 2 * TC], BF16))
        ones_tc = es_top.enter_context(nc.sbuf_tensor("ones_tc", [1, TC], BF16))
        ab1row = es_top.enter_context(nc.sbuf_tensor("ab1row", [1, H4], BF16))
        ab2row = es_top.enter_context(nc.sbuf_tensor("ab2row", [1, D], BF16))
        gatebrow = es_top.enter_context(nc.sbuf_tensor("gatebrow", [1, NE], BF16))
        gcbrow = es_top.enter_context(nc.sbuf_tensor("gcbrow", [1, 1], BF16))
        eb1row = es_top.enter_context(nc.sbuf_tensor("eb1row", [1, HM], BF16))
        h_pe = es_top.enter_context(nc.semaphore("s_pe"))
        h_act = es_top.enter_context(nc.semaphore("s_act"))
        h_dve = es_top.enter_context(nc.semaphore("s_dve"))
        h_pool = es_top.enter_context(nc.semaphore("s_pool"))
        h_gdma = es_top.enter_context(nc.semaphore("s_gdma"))
        h_ldma = es_top.enter_context(nc.semaphore("s_ldma"))
        h_sdma = es_top.enter_context(nc.semaphore("s_sdma"))
        h_cc = es_top.enter_context(nc.semaphore("s_cc"))
        h_init = es_top.enter_context(nc.semaphore("s_init"))
        h_ws = [es_top.enter_context(nc.semaphore(f"s_ws{i}")) for i in range(4)]
        h_xn = es_top.enter_context(nc.semaphore("s_xn"))
        h_eb = es_top.enter_context(nc.semaphore("s_eb"))
        h_wc = es_top.enter_context(nc.semaphore("s_wc"))
        block = es_top.enter_context(nc.Block())
        pe, act, dve, pool = Sem(h_pe), Sem(h_act), Sem(h_dve), Sem(h_pool)
        gdma, ldma, sdma, ccs = Sem(h_gdma), Sem(h_ldma), Sem(h_sdma), Sem(h_cc)
        init = Sem(h_init)
        wsems = [Sem(h) for h in h_ws]
        ebs = Sem(h_eb)
        wcs = Sem(h_wc)
        xns = Sem(h_xn)
        pp = PsumPool()

        def emit_rows(t_sums, ps_s, ps_sq):
            m_r = rows[0:1, 0:TC]
            t1_r = rows[0:1, TC:2 * TC]
            t2_r = rows[0:1, 2 * TC:3 * TC]
            mr_r = rows[0:1, 3 * TC:4 * TC]

            def r1(v, t_sums=t_sums, ps_s=ps_s, ps_sq=ps_sq):
                wait(v, t_sums)
                v.tensor_scalar(out=m_r, in0=ps_s[:, :], scalar1=1.0 / D, scalar2=None, op0=OP.mult)
                i = v.tensor_scalar(out=t1_r, in0=ps_sq[:, :], scalar1=1.0 / D, scalar2=EPS,
                                    op0=OP.mult, op1=OP.add)
                dve.inc(i)
            block.vector(r1)
            tk = dve.tok()
            pp.consumed(ps_s, tk)
            pp.consumed(ps_sq, tk)

            def r2(v, tk=tk):
                wait(v, tk)
                i = v.tensor_tensor(out=t2_r, in0=m_r, in1=m_r, op=OP.mult)
                dve.inc(i)
            block.vector(r2)
            tk = dve.tok()

            def r3(v, tk=tk):
                wait(v, tk)
                i = v.tensor_tensor(out=t1_r, in0=t1_r, in1=t2_r, op=OP.subtract)
                dve.inc(i)
            block.vector(r3)
            tk = dve.tok()

            def r4(s, tk=tk):
                wait(s, tk)
                i = s.activation(out=t2_r, in_=t1_r, func=AF.Sqrt)
                act.inc(i)
            block.scalar(r4)
            tk = act.tok()

            def r5(v, tk=tk):
                wait(v, tk)
                i = v.reciprocal(out=t1_r, in_=t2_r)
                dve.inc(i)
            block.vector(r5)
            tk = dve.tok()

            def r6(v, tk=tk):
                wait(v, tk)
                i = v.scalar_tensor_tensor(out=mr_r, in0=m_r, scalar=-1.0, in1=t1_r,
                                           op0=OP.mult, op1=OP.mult)
                dve.inc(i)
            block.vector(r6)
            tk = dve.tok()

            def r7(v, tk=tk):
                wait(v, tk)
                v.tensor_copy(out=rowbf[0:1, 0:TC], in_=t1_r)
                i = v.tensor_copy(out=rowbf[0:1, TC:2 * TC], in_=mr_r)
                dve.inc(i)
            block.vector(r7)
            return dve.tok()

        def mm_ev(n, ps_list, emit_mm, emit_ev, ev_eng="scalar", head_pe=None, head_ev=None):
            esem = act if ev_eng == "scalar" else dve
            esec = block.scalar if ev_eng == "scalar" else block.vector
            for idx in range(n):
                def fm(t, idx=idx):
                    if idx == 0:
                        for tk in (head_pe or []):
                            wait(t, tk)
                    ps = ps_list[idx % len(ps_list)]
                    pp.prod_wait(t, ps)
                    mm = emit_mm(t, ps, idx)
                    pe.inc(mm)
                block.tensor(fm)
                tokm = pe.tok()

                def fe(e, idx=idx, tokm=tokm):
                    if idx == 0:
                        for tk in (head_ev or []):
                            wait(e, tk)
                    ps = ps_list[idx % len(ps_list)]
                    wait(e, tokm)
                    i = emit_ev(e, ps, idx)
                    esem.inc(i)
                    pp.consumed(ps, esem.tok())
                esec(fe)
            return esem.tok()

        with ExitStack() as es_pro:
            prow = es_pro.enter_context(nc.sbuf_tensor("prow", [1, NE * HM], F32))
            alog_sb = es_pro.enter_context(nc.sbuf_tensor("alog_sb", [NS, CI], F32))
            pcol = es_pro.enter_context(nc.sbuf_tensor("pcol", [128, 128], F32))
            pps = es_pro.enter_context(nc.psum_tensor("pps", [128, 128], F32))
            ppsB = es_pro.enter_context(nc.psum_tensor("ppsB", [128, 128], F32))
            pps6 = es_pro.enter_context(nc.psum_tensor("pps6", [NE, 1], F32))
            def c0(v):
                v.memset(ones_bf_col[:], 1.0)
                v.memset(ones_f_row[:], 1.0)
                v.memset(ones_bf_row[:], 1.0)
                v.memset(ones_tc[:], 1.0)
                v.memset(one11[:], 1.0)
                i = v.memset(ident_f[:], 0.0)
                dve.inc(i)
            block.vector(c0)
            t_zero = dve.tok()

            def g0(g):
                wait(g, t_zero)
                for p in range(128):
                    init.inc(g.dma_start(out=ident_f[p:p + 1, p:p + 1], in_=one11[0:1, 0:1]), 16)
            block.gpsimd(g0)
            t_identf = init.tok()

            def v1(v):
                wait(v, t_identf)
                i = v.tensor_copy(out=ident_bf[:], in_=ident_f[:])
                dve.inc(i)
            block.vector(v1)
            t_ident = dve.tok()

            rounds = [
                [(d_convb, CI, convbc), (d_dp, CI, dpc),
                 (d_lng, D, lngc), (d_lnb, D, lnbc), (d_gcw, 2 * D, "gcw")],
            ]
            prev_copy = None
            prev_mm = None
            for ri in range(len(rounds)):
                def fload(s, ri=ri, prev_mm=prev_mm):
                    wait(s, prev_mm)
                    off = 0
                    for dr, ln, dst in rounds[ri]:
                        ldma.inc(s.dma_start(out=prow[0:1, off:off + ln], in_=dr.ap()[:, :]), 16)
                        off += ln
                block.sync(fload)
                t_load = ldma.tok()

                def fcolmm(t, ri=ri, t_load=t_load, prev_copy=prev_copy):
                    wait(t, t_load)
                    wait(t, t_zero)
                    wait(t, prev_copy)
                    off = 0
                    col = 0
                    mm = None
                    for dr, ln, dst in rounds[ri]:
                        for i in range(ln // 128):
                            mm = t.matmul(pps[:, col:col + 1],
                                          prow[0:1, off + i * 128:off + (i + 1) * 128],
                                          one11[:], start=True, stop=True)
                            col += 1
                        off += ln
                    pe.inc(mm)
                block.tensor(fcolmm)
                prev_mm = pe.tok()

                ncols_r = sum(ln // 128 for _, ln, _ in rounds[ri])

                def fcolevac(s, t_mm=prev_mm, ncols_r=ncols_r):
                    wait(s, t_mm)
                    i = s.copy(out=pcol[:, 0:ncols_r], in_=pps[:, 0:ncols_r])
                    act.inc(i)
                block.scalar(fcolevac)
                t_evac = act.tok()

                def fcolcopy(v, ri=ri, t_evac=t_evac):
                    wait(v, t_evac)
                    col = 0
                    last = None
                    for dr, ln, dst in rounds[ri]:
                        for i in range(ln // 128):
                            o = gcwc[:, i:i + 1] if dst == "gcw" else dst[:, i:i + 1]
                            last = v.tensor_copy(out=o, in_=pcol[:, col:col + 1])
                            col += 1
                    dve.inc(last)
                block.vector(fcolcopy)
                prev_copy = dve.tok()
            t_cols = prev_copy
            t_colmm_last = prev_mm

            def s2(s, t_colmm_last=t_colmm_last):
                wait(s, t_colmm_last)
                ldma.inc(s.dma_start(out=alog_sb[:, :], in_=d_alog.ap()[:, :]), 16)
                ldma.inc(s.dma_start(out=gcbc[:, :], in_=d_gcb.ap()[:, :]), 16)
                ldma.inc(s.dma_start(out=convwc[:, :], in_=d_convw.ap()[:, :]), 16)
            block.sync(s2)
            t_misc = ldma.tok()

            def g2(g):
                i = g.dma_start(out=gatewc[:, :], in_=d_gatew.ap()[:, :])
                gdma.inc(i, 16)
            block.gpsimd(g2)
            t_gwl = gdma.tok()

            def t3(t, t_misc=t_misc, t_cols=t_cols):
                wait(t, t_misc)
                wait(t, t_cols)
                wait(t, t_ident)
                mm = None
                for j in range(CC):
                    mm = t.transpose(ppsB[:, j * NS:(j + 1) * NS],
                                     alog_sb[0:NS, j * 128:(j + 1) * 128], ident_f[0:NS, 0:NS])
                pe.inc(mm)
            block.tensor(t3)
            t_al = pe.tok()

            def s3(s, t_al=t_al):
                wait(s, t_al)
                i = s.activation(out=pcol[:], in_=ppsB[:, :], func=AF.Exp)
                act.inc(i)
            block.scalar(s3)
            t_aexp = act.tok()

            def v3(v, t_aexp=t_aexp):
                wait(v, t_aexp)
                v.tensor_scalar(out=acols[:], in0=pcol[:], scalar1=-1.0, scalar2=None,
                                op0=OP.mult)
                i = v.memset(carry[:], 0.0)
                dve.inc(i)
            block.vector(v3)
            t_pro = dve.tok()

        def gwcast(g):
            wcs.inc(g.dma_start(out=s_aw1.ap()[:, :], in_=d_aw1.ap()[:, :]), 16)
            wcs.inc(g.dma_start(out=s_aw2.ap()[:, :], in_=d_aw2.ap()[:, :]), 16)
            for e in range(NE):
                wcs.inc(g.dma_start(out=s_ew1.ap()[e, :, :], in_=d_ew1.ap()[e, :, :]), 16)
                wcs.inc(g.dma_start(out=s_ew2.ap()[e, :, :], in_=d_ew2.ap()[e, :, :]), 16)
            wcs.inc(g.dma_start(out=s_eb1.ap()[:, :], in_=d_eb1.ap()[:, :]), 16)
        block.gpsimd(gwcast)
        t_wcast = wcs.tok()

        with ExitStack() as es_exp:
            gsb = es_exp.enter_context(nc.sbuf_tensor("gsb", [128, DK * D], BF16))
            Sa = es_exp.enter_context(nc.sbuf_tensor("Sa", [128, DK * D], BF16))
            Sb = es_exp.enter_context(nc.sbuf_tensor("Sb", [128, DK * D], BF16))
            eps0 = es_exp.enter_context(nc.psum_tensor("eps0", [128, 512], F32))
            eps1 = es_exp.enter_context(nc.psum_tensor("eps1", [128, 512], F32))
            tps0 = es_exp.enter_context(nc.psum_tensor("tps0", [128, 128], BF16))
            tps1 = es_exp.enter_context(nc.psum_tensor("tps1", [128, 128], BF16))
            def gload(g, b1=(h_dve, None), ):
                wait(g, (h_dve, dve.n))
                wait(g, (h_act, act.n))
                wait(g, (h_pe, pe.n))
                i = g.dma_start(out=gsb[:, :], in_=d_gens.ap()[:, :])
                gdma.inc(i, 16)
            block.gpsimd(gload)
            t_g = gdma.tok()

            def vS(v, t_g=t_g, t_pro=t_pro):
                wait(v, t_g)
                wait(v, t_pro)
                i = v.tensor_scalar(out=Sa[:, :], in0=gsb[:, :], scalar1=1.0 / EXPM_DEG,
                                    scalar2=None, op0=OP.mult)
                dve.inc(i)
            block.vector(vS)
            tS0 = dve.tok()

            def vSd(v, tS0=tS0):
                wait(v, tS0)
                last = None
                for m in range(DK):
                    last = v.tensor_tensor(out=Sa[:, m * D + m * 128:m * D + (m + 1) * 128],
                                           in0=Sa[:, m * D + m * 128:m * D + (m + 1) * 128],
                                           in1=ident_bf[:], op=OP.add)
                dve.inc(last)
            block.vector(vSd)
            t_prev_dve = dve.tok()

            cur, nxt = Sa, Sb
            for step in range(EXPM_DEG - 1, 0, -1):
                def emm(t, ps, idx, cur=cur):
                    m, half = divmod(idx, 2)
                    mm = None
                    for kc in range(DK):
                        mm = t.matmul(ps[:, :],
                                      gsb[:, kc * D + m * 128:kc * D + (m + 1) * 128],
                                      cur[:, kc * D + half * 512:kc * D + (half + 1) * 512],
                                      start=(kc == 0), stop=(kc == DK - 1))
                    return mm

                def eev(v, ps, idx, nxt=nxt, step=step):
                    m, half = divmod(idx, 2)
                    return v.tensor_scalar(
                        out=nxt[:, m * D + half * 512:m * D + (half + 1) * 512],
                        in0=ps[:, :], scalar1=1.0 / step, scalar2=None, op0=OP.mult)
                t_evac = mm_ev(2 * DK, [eps0, eps1], emm, eev, ev_eng="vector",
                               head_pe=[t_prev_dve])

                def vdg(v, nxt=nxt, t_evac=t_evac):
                    wait(v, t_evac)
                    last = None
                    for m in range(DK):
                        last = v.tensor_tensor(out=nxt[:, m * D + m * 128:m * D + (m + 1) * 128],
                                               in0=nxt[:, m * D + m * 128:m * D + (m + 1) * 128],
                                               in1=ident_bf[:], op=OP.add)
                    dve.inc(last)
                block.vector(vdg)
                t_prev_dve = dve.tok()
                cur, nxt = nxt, cur

            if "ET" in taps:
                def tapet(s, cur=cur, tk=t_prev_dve):
                    wait(s, tk)
                    i = s.dma_start(out=taps["ET"].ap()[:, :], in_=cur[:, :])
                    sdma.inc(i, 16)
                block.sync(tapet)

            def eship(s, cur=cur, tk=t_prev_dve):
                wait(s, tk)
                i = s.dma_start(out=bass.AP(d_ein, 0, [[D, 128], [128 * D, DK], [1, D]]),
                                in_=cur[:, :])
                sdma.inc(i, 16)
            block.sync(eship)
            t_es = sdma.tok()

            def gcoll(g, t_es=t_es):
                wait(g, t_es)
                i = g.collective_compute("AllGather", OP.bypass,
                                         replica_groups=[list(range(8))],
                                         ins=[d_ein.ap().opt()],
                                         outs=[d_eout.ap().opt()])
                ccs.inc(i, 1)
                wait(g, ccs.tok())
                i = g.dma_start(out=gsb[:, :],
                                in_=bass.AP(d_eout, 3 * D * D, [[D, 128], [128 * D, DK], [1, D]]))
                gdma.inc(i, 16)
            block.gpsimd(gcoll)
            t_e3 = gdma.tok()

            def ttm(t, ps, idx):
                rc, c2 = divmod(idx, DK)
                return t.transpose(ps[:, :], gsb[:, c2 * D + rc * 128:c2 * D + (rc + 1) * 128],
                                   ident_bf[:])

            def tte(s, ps, idx, dst=nxt):
                rc, c2 = divmod(idx, DK)
                return s.activation(out=dst[:, rc * D + c2 * 128:rc * D + (c2 + 1) * 128],
                                    in_=ps[:, :], func=AF.Copy)
            t_T = mm_ev(DK * DK, [tps0, tps1], ttm, tte, head_pe=[t_e3])

            src = nxt
            other = cur
            t_chain = t_T
            for sidx in (2, 1, 0):
                def gld(g, sidx=sidx, t_chain=t_chain):
                    wait(g, t_chain)
                    i = g.dma_start(out=gsb[:, :],
                                    in_=bass.AP(d_eout, sidx * D * D,
                                                [[D, 128], [128 * D, DK], [1, D]]))
                    gdma.inc(i, 16)
                block.gpsimd(gld)
                t_lw = gdma.tok()

                dst_t = Rmat if sidx == 0 else other

                def pmm(t, ps, idx, src=src):
                    m, half = divmod(idx, 2)
                    mm = None
                    for kc in range(DK):
                        mm = t.matmul(ps[:, :],
                                      gsb[:, kc * D + m * 128:kc * D + (m + 1) * 128],
                                      src[:, kc * D + half * 512:kc * D + (half + 1) * 512],
                                      start=(kc == 0), stop=(kc == DK - 1))
                    return mm

                def pev(s, ps, idx, dst_t=dst_t):
                    m, half = divmod(idx, 2)
                    return s.activation(
                        out=dst_t[:, m * D + half * 512:m * D + (half + 1) * 512],
                        in_=ps[:, :], func=AF.Copy)
                t_chain = mm_ev(2 * DK, [eps0, eps1], pmm, pev, head_pe=[t_lw])
                src, other = dst_t, src
            t_R = t_chain

            if "R" in taps:
                def tapr(s, t_R=t_R):
                    wait(s, t_R)
                    i = s.dma_start(out=taps["R"].ap()[:, :], in_=Rmat[:, :])
                    sdma.inc(i, 16)
                block.sync(tapr)

        with ExitStack() as es_att:
            win_sb = es_att.enter_context(nc.sbuf_tensor("win_sb", [128, DK * 2 * CI], BF16))
            wout_sb = es_att.enter_context(nc.sbuf_tensor("wout_sb", [128, CC * D], BF16))
            xb0 = es_att.enter_context(nc.sbuf_tensor("xb0", [128, DK * TC], BF16))
            xb1 = es_att.enter_context(nc.sbuf_tensor("xb1", [128, DK * TC], BF16))
            n1a = es_att.enter_context(nc.sbuf_tensor("n1a", [128, DK * TC], BF16))
            n1b = es_att.enter_context(nc.sbuf_tensor("n1b", [128, DK * TC], BF16))
            u_raw = es_att.enter_context(nc.sbuf_tensor("u_raw", [128, CC * (TC + 3)], BF16))
            ucb0 = es_att.enter_context(nc.sbuf_tensor("ucb0", [128, CC * TC], BF16))
            ucb1 = es_att.enter_context(nc.sbuf_tensor("ucb1", [128, CC * TC], BF16))
            cvt0 = es_att.enter_context(nc.sbuf_tensor("cvt0", [128, 4 * TC], BF16))
            szb = es_att.enter_context(nc.sbuf_tensor("szb", [128, CC * TC], BF16))
            scano = es_att.enter_context(nc.sbuf_tensor("scano", [128, 8 * TC], BF16))
            pairb = es_att.enter_context(nc.sbuf_tensor("pairb", [128, 4 * TC], BF16))
            quadb = es_att.enter_context(nc.sbuf_tensor("quadb", [128, 3 * TC], BF16))
            ybuf = es_att.enter_context(nc.sbuf_tensor("ybuf", [128, TC], BF16))
            ygbuf = es_att.enter_context(nc.sbuf_tensor("ygbuf", [128, CC * TC], BF16))
            apart = es_att.enter_context(nc.sbuf_tensor("apart", [128, DK * TC], F32))
            bcastt = es_att.enter_context(nc.sbuf_tensor("bcastt", [128, 2 * TC], BF16))
            sps0 = es_att.enter_context(nc.psum_tensor("sps0", [1, TC], F32))
            sps1 = es_att.enter_context(nc.psum_tensor("sps1", [1, TC], F32))
            bps = es_att.enter_context(nc.psum_tensor("bps", [128, TC], F32))
            wps0 = es_att.enter_context(nc.psum_tensor("wps0", [128, TC], F32))
            wps1 = es_att.enter_context(nc.psum_tensor("wps1", [128, TC], F32))
            ops0 = es_att.enter_context(nc.psum_tensor("ops0", [128, TC], F32))
            ops1 = es_att.enter_context(nc.psum_tensor("ops1", [128, TC], F32))
            def wload(g, bar=None):
                wait(g, (h_sdma, sdma.n))
                wait(g, (h_dve, dve.n))
                wait(g, (h_act, act.n))
                wait(g, (h_pe, pe.n))
                gdma.inc(g.dma_start(out=win_sb[:, :], in_=d_win.ap()[:, :]), 16)
                gdma.inc(g.dma_start(out=wout_sb[:, :], in_=d_wout.ap()[:, :]), 16)
            block.gpsimd(wload)
            t_w = gdma.tok()

            def uzero(v, bar=None):
                wait(v, (h_sdma, sdma.n))
                wait(v, (h_act, act.n))
                wait(v, (h_pe, pe.n))
                i = v.memset(u_raw[:, :], 0.0)
                dve.inc(i)
            block.vector(uzero)
            t_uzero = dve.tok()

            xbufs = [xb0, xb1]
            n1bufs = [n1a, n1b]
            t_xload = [None] * (NCA + 1)
            xb_free = [None, None]
            n1_free = [None, None]
            t_halo = [None] * (NCA + 1)
            t_halo[0] = t_uzero
            t_yg = [None] * NCA
            t_apart_dma = [None] * NCA

            def emit_xload(k):
                def f(g, k=k, ebar=(h_act, act.n), gprev=(h_gdma, gdma.n)):
                    wait(g, xb_free[k % 2])
                    if k < 2:
                        wait(g, ebar)
                        wait(g, (h_pe, pe.n) if k < 2 else None)
                        wait(g, gprev)
                    i = g.dma_start(out=xbufs[k % 2][:, :],
                                    in_=bass.AP(d_xT, k * 128 * DKTC, [[DKTC, 128], [1, DKTC]]))
                    gdma.inc(i, 16)
                block.gpsimd(f)
                t_xload[k] = gdma.tok()

            emit_xload(0)

            for k in range(NCA):
                xb = xbufs[k % 2]
                n1 = n1bufs[k % 2]
                if k + 1 < NCA:
                    emit_xload(k + 1)

                def fsq(s, k=k, xb=xb, n1=n1):
                    wait(s, t_xload[k])
                    wait(s, n1_free[k % 2])
                    wait(s, t_apart_dma[k - 1] if k > 0 else None)
                    i = s.activation(out=n1[:, :], in_=xb[:, :], func=AF.Square)
                    act.inc(i)
                block.scalar(fsq)
                t_sq = act.tok()

                def fstat(t, k=k, xb=xb, n1=n1, t_sq=t_sq):
                    wait(t, t_sq)
                    if k == 0:
                        wait(t, t_w)
                    pp.prod_wait(t, sps0)
                    mm = None
                    for dk in range(DK):
                        mm = t.matmul(sps0[:, :], ones_bf_col[:], xb[:, dk * TC:(dk + 1) * TC],
                                      start=(dk == 0), stop=(dk == DK - 1))
                    pe.inc(mm)
                    pp.prod_wait(t, sps1)
                    mm = None
                    for dk in range(DK):
                        mm = t.matmul(sps1[:, :], ones_bf_col[:], n1[:, dk * TC:(dk + 1) * TC],
                                      start=(dk == 0), stop=(dk == DK - 1))
                    pe.inc(mm)
                block.tensor(fstat)
                t_sums = pe.tok()

                t_rows_bf = emit_rows(t_sums, sps0, sps1)

                def fbc(t, t_rows_bf=t_rows_bf):
                    wait(t, t_rows_bf)
                    pp.prod_wait(t, bps)
                    i = t.matmul(bps[:, :], ones_bf_row[:, :], rowbf[0:1, 0:TC],
                                 start=True, stop=True)
                    pe.inc(i)
                block.tensor(fbc)
                t_b1 = pe.tok()

                def fbce(s, t_b1=t_b1):
                    wait(s, t_b1)
                    i = s.activation(out=bcastt[:, 0:TC], in_=bps[:, :], func=AF.Copy)
                    act.inc(i)
                block.scalar(fbce)
                t_b1e = act.tok()
                pp.consumed(bps, t_b1e)

                def fbc2(t, t_b1e=t_b1e):
                    pp.prod_wait(t, bps)
                    i = t.matmul(bps[:, :], ones_bf_row[:, :], rowbf[0:1, TC:2 * TC],
                                 start=True, stop=True)
                    pe.inc(i)
                block.tensor(fbc2)
                t_b2 = pe.tok()

                def fbc2e(s, t_b2=t_b2):
                    wait(s, t_b2)
                    i = s.activation(out=bcastt[:, TC:2 * TC], in_=bps[:, :], func=AF.Copy)
                    act.inc(i)
                block.scalar(fbc2e)
                t_bc = act.tok()
                pp.consumed(bps, t_bc)

                def fn1(v, xb=xb, n1=n1, k=k, t_bc=t_bc):
                    wait(v, t_bc)
                    wait(v, n1_free[k % 2])
                    i = None
                    for dk in range(DK):
                        i = v.tensor_tensor(out=n1[:, dk * TC:(dk + 1) * TC],
                                            in0=xb[:, dk * TC:(dk + 1) * TC],
                                            in1=bcastt[:, 0:TC], op=OP.mult)
                    dve.inc(i)
                block.vector(fn1)
                t_n1h = dve.tok()

                def fn1b(v, n1=n1, t_n1h=t_n1h):
                    wait(v, t_n1h)
                    last = None
                    for dk in range(DK):
                        last = v.tensor_tensor(out=n1[:, dk * TC:(dk + 1) * TC],
                                               in0=n1[:, dk * TC:(dk + 1) * TC],
                                               in1=bcastt[:, TC:2 * TC], op=OP.add)
                    dve.inc(last)
                block.vector(fn1b)
                t_n1r = dve.tok()
                xb_free[k % 2] = t_n1r

                if "n1" in taps and k == 0:
                    def tapn1(s, n1=n1, tk=t_n1r):
                        wait(s, tk)
                        i = s.dma_start(out=taps["n1"].ap()[:, :], in_=n1[:, :])
                        sdma.inc(i, 16)
                    block.sync(tapn1)

                def wmm(t, ps, ccx, n1=n1):
                    mm = None
                    for dk in range(DK):
                        mm = t.matmul(ps[:, :],
                                      win_sb[:, dk * 2 * CI + ccx * 128:dk * 2 * CI + (ccx + 1) * 128],
                                      n1[:, dk * TC:(dk + 1) * TC],
                                      start=(dk == 0), stop=(dk == DK - 1))
                    return mm

                def wev(s, ps, ccx):
                    if ccx < CC:
                        return s.activation(out=u_raw[:, ccx * (TC + 3) + 3:(ccx + 1) * (TC + 3)],
                                            in_=ps[:, :], func=AF.Copy)
                    cz = ccx - CC
                    return s.activation(out=szb[:, cz * TC:(cz + 1) * TC],
                                        in_=ps[:, :], func=AF.Silu)
                t_wevac = mm_ev(16, [wps0, wps1], wmm, wev,
                                head_pe=[t_n1r, t_halo[k],
                                         t_yg[k - 1] if k > 0 else None])
                n1_free[k % 2] = pe.tok()

                ucb = [ucb0, ucb1][k % 2]

                def fconv(g, k=k, t_wevac=t_wevac, ucb=ucb):
                    wait(g, t_wevac)
                    wait(g, t_yg[k - 2] if k > 1 else None)
                    wait(g, t_apart_dma[k - 1] if k > 0 else None)
                    for ccx in range(CC):
                        base = ccx * (TC + 3)
                        o = ucb[:, ccx * TC:(ccx + 1) * TC]
                        for tp in range(4):
                            w0 = bass.AP(convwc, ccx * 4 + tp, [[CC * 4, 128], [0, TC]])
                            i = g.tensor_tensor(out=cvt0[:, tp * TC:(tp + 1) * TC],
                                                in0=u_raw[:, base + tp:base + tp + TC],
                                                in1=w0, op=OP.mult)
                        pool.inc(i)
                        wait(g, pool.tok())
                        i = g.tensor_tensor(out=cvt0[:, 0:TC], in0=cvt0[:, 0:TC],
                                            in1=cvt0[:, TC:2 * TC], op=OP.add)
                        i = g.tensor_tensor(out=cvt0[:, 2 * TC:3 * TC],
                                            in0=cvt0[:, 2 * TC:3 * TC],
                                            in1=cvt0[:, 3 * TC:4 * TC], op=OP.add)
                        pool.inc(i)
                        wait(g, pool.tok())
                        i = g.tensor_tensor(out=cvt0[:, 0:TC], in0=cvt0[:, 0:TC],
                                            in1=cvt0[:, 2 * TC:3 * TC], op=OP.add)
                        pool.inc(i)
                        wait(g, pool.tok())
                        b0 = bass.AP(convbc, ccx, [[CC, 128], [0, TC]])
                        i = g.tensor_tensor(out=o, in0=cvt0[:, 0:TC], in1=b0, op=OP.add)
                        pool.inc(i)
                        wait(g, pool.tok())
                block.gpsimd(fconv)
                t_uc = pool.tok()

                def fhalo(g, tpr=pool.tok()):
                    wait(g, tpr)
                    last = None
                    for ccx in range(CC):
                        base = ccx * (TC + 3)
                        last = g.tensor_copy(out=u_raw[:, base + 0:base + 3],
                                             in_=u_raw[:, base + TC:base + TC + 3])
                    pool.inc(last)
                block.gpsimd(fhalo)
                t_halo[k + 1] = pool.tok()

                if "uc" in taps and k == 0:
                    def tapuc(s, tk=t_uc):
                        wait(s, tk)
                        i = s.dma_start(out=taps["uc"].ap()[:, :], in_=ucb[:, :])
                        sdma.inc(i, 16)
                    block.sync(tapuc)

                def fscan(v, k=k, t_uc=t_uc, ucb=ucb):
                    wait(v, t_uc)
                    if k == 0:
                        wait(v, t_pro)
                    for ccx in range(CC):
                        for half in range(2):
                            last = None
                            for i2 in range(8):
                                sx = half * 8 + i2
                                a0 = bass.AP(acols, ccx * NS + sx, [[128, 128], [0, TC]])
                                last = v.tensor_tensor_scan(
                                    out=scano[:, i2 * TC:(i2 + 1) * TC],
                                    data0=a0, data1=ucb[:, ccx * TC:(ccx + 1) * TC],
                                    initial=carry[:, ccx * NS + sx:ccx * NS + sx + 1],
                                    op0=OP.mult, op1=OP.add)
                            dve.inc(last)
                            wait(v, dve.tok())
                            v.tensor_copy(out=carry[:, ccx * NS + half * 8:ccx * NS + half * 8 + 8],
                                          in_=bass.AP(scano, TC - 1, [[8 * TC, 128], [TC, 8]]))
                            last = None
                            for i2 in range(4):
                                last = v.tensor_tensor(out=pairb[:, i2 * TC:(i2 + 1) * TC],
                                                       in0=scano[:, (2 * i2) * TC:(2 * i2 + 1) * TC],
                                                       in1=scano[:, (2 * i2 + 1) * TC:(2 * i2 + 2) * TC],
                                                       op=OP.add)
                            dve.inc(last)
                            wait(v, dve.tok())
                            v.tensor_tensor(out=quadb[:, 0:TC], in0=pairb[:, 0:TC],
                                            in1=pairb[:, TC:2 * TC], op=OP.add)
                            i3 = v.tensor_tensor(out=quadb[:, TC:2 * TC], in0=pairb[:, 2 * TC:3 * TC],
                                                 in1=pairb[:, 3 * TC:4 * TC], op=OP.add)
                            dve.inc(i3)
                            wait(v, dve.tok())
                            dsthalf = ybuf[:, 0:TC] if half == 0                                 else quadb[:, 2 * TC:3 * TC]
                            i3 = v.tensor_tensor(out=dsthalf, in0=quadb[:, 0:TC],
                                                 in1=quadb[:, TC:2 * TC], op=OP.add)
                            dve.inc(i3)
                            wait(v, dve.tok())
                            if half == 1:
                                i3 = v.tensor_tensor(out=ybuf[:, 0:TC],
                                                     in0=ybuf[:, 0:TC],
                                                     in1=quadb[:, 2 * TC:3 * TC], op=OP.add)
                                dve.inc(i3)
                                wait(v, dve.tok())
                        i3 = v.scalar_tensor_tensor(out=ybuf[:, 0:TC],
                                                    in0=ucb[:, ccx * TC:(ccx + 1) * TC],
                                                    scalar=dpc[:, ccx:ccx + 1],
                                                    in1=ybuf[:, 0:TC],
                                                    op0=OP.mult, op1=OP.add)
                        dve.inc(i3)
                        wait(v, dve.tok())
                        i3 = v.tensor_tensor(out=ygbuf[:, ccx * TC:(ccx + 1) * TC],
                                             in0=ybuf[:, 0:TC],
                                             in1=szb[:, ccx * TC:(ccx + 1) * TC], op=OP.mult)
                        dve.inc(i3)
                        wait(v, dve.tok())
                block.vector(fscan)
                t_yg[k] = dve.tok()

                if "yg" in taps and k == 0:
                    def tapyg(s, tk=t_yg[k]):
                        wait(s, tk)
                        i = s.dma_start(out=taps["yg"].ap()[:, :], in_=ygbuf[:, :])
                        sdma.inc(i, 16)
                    block.sync(tapyg)
                if "y" in taps and k == 0:
                    def tapy(s, tk=t_y2):
                        wait(s, tk)
                        i = s.dma_start(out=taps["y"].ap()[:, :], in_=ybuf[:, :])
                        sdma.inc(i, 16)
                    block.sync(tapy)

                def omm(t, ps, dj):
                    mm = None
                    for ci in range(CC):
                        mm = t.matmul(ps[:, :],
                                      wout_sb[:, ci * D + dj * 128:ci * D + (dj + 1) * 128],
                                      ygbuf[:, ci * TC:(ci + 1) * TC],
                                      start=(ci == 0), stop=(ci == CC - 1))
                    return mm

                def oev(s, ps, dj):
                    return s.activation(out=apart[:, dj * TC:(dj + 1) * TC],
                                        in_=ps[:, :], func=AF.Copy)
                t_ap = mm_ev(DK, [ops0, ops1], omm, oev,
                             head_pe=[t_yg[k], t_apart_dma[k - 1] if k > 0 else None])

                if "apart" in taps and k == 0:
                    def tapap(s, tk=t_ap):
                        wait(s, tk)
                        i = s.dma_start(out=taps["apart"].ap()[:, :], in_=apart[:, :])
                        sdma.inc(i, 16)
                    block.sync(tapap)

                def fastore(s, k=k, t_ap=t_ap):
                    wait(s, t_ap)
                    i = s.dma_start(out=bass.AP(d_aloc, k * 128 * DKTC, [[DKTC, 128], [1, DKTC]]),
                                    in_=apart[:, :])
                    sdma.inc(i, 16)
                block.sync(fastore)
                t_apart_dma[k] = sdma.tok()

            t_attn_dma = sdma.tok()

        if cfg.get("stop") == "attn":
            def fend(s, tk=t_attn_dma):
                wait(s, tk)
            block.sync(fend)
            return nc

        def fxchg(g, t_attn_dma=t_attn_dma):
            wait(g, t_attn_dma)
            pid = g.partition_id()
            parity = g.snap(pid % 2, min_val=0, max_val=1)
            poff = g.snap(parity * (NCA * 128 * DKTC), min_val=0, max_val=NCA * 128 * DKTC)
            toff = g.snap(parity * (NCF * 128 * DKTC), min_val=0, max_val=NCF * 128 * DKTC)
            i = g.dma_start(out=bass.AP(shb, poff, [[128 * DKTC, NCA], [DKTC, 128], [1, DKTC]]),
                            in_=bass.AP(d_aloc, 0, [[128 * DKTC, NCA], [DKTC, 128], [1, DKTC]]))
            gdma.inc(i, 16)
            gdma.inc(g.dma_start(out=d_bin.ap()[:, :], in_=ones_f_row[0:1, 0:8]), 16)
            wait(g, gdma.tok())
            i = g.collective_compute("AllGather", OP.bypass,
                                     replica_groups=[[0, 1], [2, 3], [4, 5], [6, 7]],
                                     ins=[d_bin.ap().opt()],
                                     outs=[d_bout.ap().opt()])
            ccs.inc(i, 1)
            wait(g, ccs.tok())
            i = g.dma_start(out=bass.AP(d_ax, 0, [[128 * DKTC, NCF], [DKTC, 128], [1, DKTC]]),
                            in_=bass.AP(shb, toff, [[128 * DKTC, NCF], [DKTC, 128], [1, DKTC]]))
            gdma.inc(i, 16)
            i = g.dma_start(out=bass.AP(d_ax, NCF * 128 * DKTC,
                                        [[128 * DKTC, NCF], [DKTC, 128], [1, DKTC]]),
                            in_=bass.AP(shb, toff + NCA * 128 * DKTC,
                                        [[128 * DKTC, NCF], [DKTC, 128], [1, DKTC]]))
            gdma.inc(i, 16)
        block.gpsimd(fxchg)
        t_ax_ready = gdma.tok()

        if cfg.get("stop") == "xchg":
            def fend2(g, tk=t_ax_ready):
                wait(g, tk)
            block.gpsimd(fend2)
            return nc

        with ExitStack() as es_ffn:
            xn2_sb = es_ffn.enter_context(nc.sbuf_tensor("xn2sb", [128, DK * TC], BF16))
            ldbuf = es_ffn.enter_context(nc.sbuf_tensor("ldbuf", [128, DK * TC], F32))
            t32a = es_ffn.enter_context(nc.sbuf_tensor("t32a", [128, DK * TC], F32))
            t32b = es_ffn.enter_context(nc.sbuf_tensor("t32b", [128, DK * TC], F32))
            hbuf = es_ffn.enter_context(nc.sbuf_tensor("hbuf", [128, 32 * TC], BF16))
            wstr0 = es_ffn.enter_context(nc.sbuf_tensor("wstr0", [128, 4096], BF16))
            wstr1 = es_ffn.enter_context(nc.sbuf_tensor("wstr1", [128, 4096], BF16))
            wstr2 = es_ffn.enter_context(nc.sbuf_tensor("wstr2", [128, 4096], BF16))
            abel_bf = es_ffn.enter_context(nc.sbuf_tensor("abel_bf", [128, DK * TC], BF16))
            nonab_bf = es_ffn.enter_context(nc.sbuf_tensor("nonab_bf", [128, DK * TC], BF16))
            accf = es_ffn.enter_context(nc.sbuf_tensor("accf", [128, DK * TC], F32))
            wT6 = es_ffn.enter_context(nc.sbuf_tensor("wT6", [NE, TC], BF16))
            lgrow = es_ffn.enter_context(nc.sbuf_tensor("lgrow", [NE, TC], F32))
            wbc = es_ffn.enter_context(nc.sbuf_tensor("wbc", [128, NE * TC], BF16))
            wscr = es_ffn.enter_context(nc.sbuf_tensor("wscr", [128, 96], F32))
            lgT = es_ffn.enter_context(nc.sbuf_tensor("lgT", [128, 32], F32))
            ttiny = es_ffn.enter_context(nc.sbuf_tensor("ttiny", [128, 32], F32))
            fbct = es_ffn.enter_context(nc.sbuf_tensor("fbct", [128, 3 * TC], F32))
            selrow = es_ffn.enter_context(nc.sbuf_tensor("selrow", [NE, NE * 128], BF16))
            eb2r = es_ffn.enter_context(nc.sbuf_tensor("eb2r", [NE, D], BF16))
            fps0 = es_ffn.enter_context(nc.psum_tensor("fps0", [128, TC], F32))
            fps1 = es_ffn.enter_context(nc.psum_tensor("fps1", [128, TC], F32))
            gps0 = es_ffn.enter_context(nc.psum_tensor("gps0", [128, TC], F32))
            gps1 = es_ffn.enter_context(nc.psum_tensor("gps1", [128, TC], F32))
            rps = es_ffn.enter_context(nc.psum_tensor("rps", [1, TC], F32))
            rps2 = es_ffn.enter_context(nc.psum_tensor("rps2", [1, TC], F32))
            lps = es_ffn.enter_context(nc.psum_tensor("lps", [NE, TC], F32))
            wtp = es_ffn.enter_context(nc.psum_tensor("wtp", [NE, TC], F32))

            def fsel(v, b=None):
                wait(v, (h_sdma, sdma.n))
                wait(v, (h_pool, pool.n))
                wait(v, (h_act, act.n))
                wait(v, (h_pe, pe.n))
                wait(v, (h_gdma, gdma.n))
                i = v.memset(selrow[:], 0.0)
                dve.inc(i)
            block.vector(fsel)
            t_sel0 = dve.tok()

            def gsel(g, t_sel0=t_sel0):
                wait(g, t_sel0)
                for e in range(NE):
                    init.inc(g.dma_start(out=selrow[e:e + 1, e * 128:(e + 1) * 128],
                                         in_=ones_bf_row[0:1, :]), 16)
            block.gpsimd(gsel)
            t_sel = init.tok()

            def gbias(g):
                ebs.inc(g.dma_start(out=ab1row[:, :], in_=d_ab1.ap()[:, :]), 16)
                ebs.inc(g.dma_start(out=ab2row[:, :], in_=d_ab2.ap()[:, :]), 16)
                ebs.inc(g.dma_start(out=gatebrow[:, :], in_=d_gateb.ap()[:, :]), 16)
                ebs.inc(g.dma_start(out=gcbrow[:, :], in_=d_gcb.ap()[:, :]), 16)
            block.gpsimd(gbias)
            t_brow = ebs.tok()

            def leb2(g, b=None):
                wait(g, (h_pool, pool.n))
                wait(g, (h_dve, dve.n))
                wait(g, (h_act, act.n))
                wait(g, (h_pe, pe.n))
                wait(g, (h_eb, ebs.n))
                i = g.dma_start(out=eb2r[:, :], in_=d_eb2.ap()[:, :])
                ebs.inc(i, 16)
            block.gpsimd(leb2)
            t_eb2 = ebs.tok()

            def ln_stats(t_src_ready):
                def fsq2(s, tk=t_src_ready):
                    wait(s, tk)
                    i = s.activation(out=hbuf[:, 0:DK * TC], in_=hbuf[:, DK * TC:2 * DK * TC],
                                     func=AF.Square)
                    act.inc(i)
                block.scalar(fsq2)
                t_sq = act.tok()

                def fmm(t, t_sq=t_sq):
                    wait(t, t_sq)
                    pp.prod_wait(t, rps)
                    mm = None
                    for dk in range(DK):
                        mm = t.matmul(rps[:, :], ones_bf_col[:],
                                      hbuf[:, DK * TC + dk * TC:DK * TC + (dk + 1) * TC],
                                      start=(dk == 0), stop=(dk == DK - 1))
                    pe.inc(mm)
                    pp.prod_wait(t, rps2)
                    mm = None
                    for dk in range(DK):
                        mm = t.matmul(rps2[:, :], ones_bf_col[:], hbuf[:, dk * TC:(dk + 1) * TC],
                                      start=(dk == 0), stop=(dk == DK - 1))
                    pe.inc(mm)
                block.tensor(fmm)
                t_mm = pe.tok()

                tk = emit_rows(t_mm, rps, rps2)

                def fb1(t, tk=tk):
                    wait(t, tk)
                    pp.prod_wait(t, gps0)
                    i = t.matmul(gps0[:, :], ones_f_row[:, :], rows[0:1, TC:2 * TC],
                                 start=True, stop=True)
                    pe.inc(i)
                    pp.prod_wait(t, gps1)
                    i = t.matmul(gps1[:, :], ones_f_row[:, :], rows[0:1, 3 * TC:4 * TC],
                                 start=True, stop=True)
                    pe.inc(i)
                block.tensor(fb1)
                tk = pe.tok()

                def fb2(s, tk=tk):
                    wait(s, tk)
                    s.activation(out=fbct[:, 0:TC], in_=gps0[:, :], func=AF.Copy)
                    i = s.activation(out=fbct[:, TC:2 * TC], in_=gps1[:, :], func=AF.Copy)
                    act.inc(i)
                block.scalar(fb2)
                tk = act.tok()
                pp.consumed(gps0, tk)
                pp.consumed(gps1, tk)
                return tk

            def normalize(dst, src, tk_st, extra=None, t_extra=None):
                def fna(v, tk_st=tk_st, dst=dst, src=src, tsd=None):
                    wait(v, tk_st)
                    wait(v, (h_sdma, sdma.n))
                    i = None
                    for dk in range(DK):
                        i = v.tensor_tensor(out=dst[:, dk * TC:(dk + 1) * TC],
                                            in0=src[:, dk * TC:(dk + 1) * TC],
                                            in1=fbct[:, 0:TC], op=OP.mult)
                    dve.inc(i)
                block.vector(fna)
                tk = dve.tok()

                def fnb(v, tk=tk, dst=dst):
                    wait(v, tk)
                    i = None
                    for dk in range(DK):
                        i = v.tensor_tensor(out=dst[:, dk * TC:(dk + 1) * TC],
                                            in0=dst[:, dk * TC:(dk + 1) * TC],
                                            in1=fbct[:, TC:2 * TC], op=OP.add)
                    dve.inc(i)
                block.vector(fnb)
                tk = dve.tok()
                if extra is not None:
                    def fnc(v, tk=tk, t_extra=t_extra, dst=dst, extra=extra):
                        wait(v, tk)
                        wait(v, t_extra)
                        i = None
                        for dk in range(DK):
                            i = v.tensor_tensor(out=dst[:, dk * TC:(dk + 1) * TC],
                                                in0=dst[:, dk * TC:(dk + 1) * TC],
                                                in1=extra[:, dk * TC:(dk + 1) * TC], op=OP.add)
                        dve.inc(i)
                    block.vector(fnc)
                    tk = dve.tok()
                return tk

            def copy_sbf(src, tk_src):
                def f(v, tk_src=tk_src, src=src):
                    wait(v, tk_src)
                    i = v.tensor_copy(out=hbuf[:, DK * TC:2 * DK * TC], in_=src)
                    dve.inc(i)
                block.vector(f)
                return dve.tok()

            t_ld_free = None
            t_a_free = None
            t_prev_xn2dma = None
            for kf in range(NCF):
                def fload_a(g, kf=kf, tfree=t_ld_free, tfa=t_a_free):
                    wait(g, t_ax_ready)
                    wait(g, tfree)
                    wait(g, tfa)
                    gdma.inc(g.dma_start(out=ldbuf[:, 0:DK * TC],
                             in_=bass.AP(d_ax, kf * 128 * DKTC, [[DKTC, 128], [1, DKTC]])), 16)
                    gdma.inc(g.dma_start(out=t32a[:, :],
                             in_=bass.AP(d_ax, (NCF + kf) * 128 * DKTC,
                                         [[DKTC, 128], [1, DKTC]])), 16)
                block.gpsimd(fload_a)
                t_lda0 = gdma.tok()

                def fadda(v, tk=t_lda0):
                    wait(v, tk)
                    i = v.tensor_tensor(out=t32a[:, :], in0=t32a[:, :], in1=ldbuf[:, 0:DK * TC],
                                        op=OP.add)
                    dve.inc(i)
                block.vector(fadda)
                t_lda = dve.tok()

                def fload_x0(s, kf=kf, t_lda=t_lda):
                    wait(s, t_lda)
                    if kf == 0:
                        wait(s, (h_ldma, ldma.n))
                    i = s.dma_start(out=ldbuf[:, 0:DK * TC],
                                    in_=bass.AP(d_x0T, kf * 128 * DKTC, [[DKTC, 128], [1, DKTC]]))
                    ldma.inc(i, 16)
                block.sync(fload_x0)
                t_ldx0 = ldma.tok()

                a_sl = t32a[:, :]
                x0_sl = ldbuf[:, 0:DK * TC]

                t_abf = copy_sbf(a_sl, t_lda)
                tk_st = ln_stats(t_abf)
                t_x1 = normalize(t32b, a_sl, tk_st, extra=x0_sl, t_extra=t_ldx0)
                t_ld_free = t_x1

                if "x1c" in taps and kf == 0:
                    def tapx1(s, tk=t_x1):
                        wait(s, tk)
                        i = s.dma_start(out=taps["x1c"].ap()[:, :], in_=t32b[:, :])
                        sdma.inc(i, 16)
                    block.sync(tapx1)

                def fx1s(s, kf=kf, t_x1=t_x1):
                    wait(s, t_x1)
                    i = s.dma_start(out=bass.AP(d_x1, kf * 128 * DKTC, [[DKTC, 128], [1, DKTC]]),
                                    in_=t32b[:, :])
                    sdma.inc(i, 16)
                block.sync(fx1s)

                t_x1bf = copy_sbf(t32b[:, :], t_x1)
                tk_st2 = ln_stats(t_x1bf)
                t_n2 = normalize(t32a, t32b, tk_st2)

                def fn2s(s, kf=kf, t_n2=t_n2):
                    wait(s, t_n2)
                    i = s.dma_start(out=bass.AP(d_n2, kf * 128 * DKTC, [[DKTC, 128], [1, DKTC]]),
                                    in_=t32a[:, :])
                    sdma.inc(i, 16)
                block.sync(fn2s)

                t_n2bf = copy_sbf(t32a[:, :], t_n2)
                tk_st3 = ln_stats(t_n2bf)
                t_xpre = normalize(t32b, t32a, tk_st3)

                def fxn2c(v, t_xpre=t_xpre):
                    wait(v, t_xpre)
                    wait(v, (h_sdma, sdma.n))
                    last = None
                    for dk in range(DK):
                        last = v.tensor_scalar(out=xn2_sb[:, dk * TC:(dk + 1) * TC],
                                               in0=t32b[:, dk * TC:(dk + 1) * TC],
                                               scalar1=lngc[:, dk:dk + 1], scalar2=lnbc[:, dk:dk + 1],
                                               op0=OP.mult, op1=OP.add)
                    dve.inc(last)
                block.vector(fxn2c)
                t_xn2v = dve.tok()
                t_a_free = t_xn2v

                if "xn2" in taps and kf == 0:
                    def tapxn2(s, tk=t_xn2v):
                        wait(s, tk)
                        i = s.dma_start(out=taps["xn2"].ap()[:, :], in_=xn2_sb[:, :])
                        sdma.inc(i, 16)
                    block.sync(tapxn2)

                def fxn2s(s, kf=kf, tk=t_xn2v):
                    wait(s, tk)
                    i = s.dma_start(out=bass.AP(d_xn2, kf * 128 * DKTC, [[DKTC, 128], [1, DKTC]]),
                                    in_=xn2_sb[:, :])
                    sdma.inc(i, 16)
                block.sync(fxn2s)
                t_prev_xn2dma = sdma.tok()
            t_bound_done = t_prev_xn2dma
            if cfg.get("stop") == "bound":
                def fend3(s, tk=t_bound_done):
                    wait(s, tk)
                block.sync(fend3)
                return nc

            wbufs = [wstr0, wstr1, wstr2]
            wb_free = [t_wcast, t_wcast, t_wcast]
            wslot = [0]

            def stream_w(dr_ap, ncols):
                sl = wslot[0] % 3
                wslot[0] += 1
                buf = wbufs[sl]
                sem = wsems[sl]

                def f(s, sl=sl, dr_ap=dr_ap, ncols=ncols, buf=buf, sem=sem):
                    wait(s, wb_free[sl])
                    i = s.dma_start(out=buf[:, 0:ncols], in_=dr_ap)
                    sem.inc(i, 16)
                block.scalar(f)
                return buf, sl, sem.tok()

            prev_st4 = None
            t_moe_prev = None
            prev_xn2_pe = None
            for kf in range(NCF):
                def fxl(g, kf=kf, tb=t_bound_done, tfree=prev_xn2_pe,
                        tdma=None):
                    wait(g, tb)
                    wait(g, tfree)
                    wait(g, (h_sdma, sdma.n))
                    if kf == 0:
                        wait(g, (h_gdma, gdma.n))
                    i = g.dma_start(out=xn2_sb[:, :],
                                    in_=bass.AP(d_xn2, kf * 128 * DKTC, [[DKTC, 128], [1, DKTC]]))
                    xns.inc(i, 16)
                block.gpsimd(fxl)
                t_xk = xns.tok()
                xk = xn2_sb

                for hb in range(8):
                    wtile, sl, t_wl = stream_w(s_aw1.ap()[hb * 128:(hb + 1) * 128, :], 4096)

                    def h1m(t, ps, sub, wtile=wtile, hb=hb):
                        hc = hb * 4 + sub
                        for dk in range(DK):
                            t.matmul(ps[:, :],
                                     wtile[:, dk * 512 + sub * 128:dk * 512 + (sub + 1) * 128],
                                     xk[:, dk * TC:(dk + 1) * TC],
                                     start=(dk == 0), stop=False)
                        return t.matmul(ps[:, :], ab1row[0:1, hc * 128:(hc + 1) * 128],
                                        ones_tc[:, :], start=False, stop=True)

                    def h1e(s, ps, sub, hb=hb):
                        hc = hb * 4 + sub
                        return s.activation(out=hbuf[:, hc * TC:(hc + 1) * TC], in_=ps[:, :],
                                            func=AF.Gelu)
                    mm_ev(4, [fps0, fps1], h1m, h1e,
                          head_pe=[t_wl, t_xk, t_brow if (hb == 0 and kf == 0) else None])
                    wb_free[sl] = pe.tok()
                t_h1 = act.tok()

                for dj in range(DK):
                    wtile, sl, t_wl = stream_w(s_aw2.ap()[dj * 128:(dj + 1) * 128, :], 4096)

                    def abm(t, ps, _i, wtile=wtile, dj=dj):
                        for kc in range(32):
                            t.matmul(ps[:, :], wtile[:, kc * 128:(kc + 1) * 128],
                                     hbuf[:, kc * TC:(kc + 1) * TC],
                                     start=(kc == 0), stop=False)
                        return t.matmul(ps[:, :], ab2row[0:1, dj * 128:(dj + 1) * 128],
                                        ones_tc[:, :], start=False, stop=True)

                    def abe(s, ps, _i, dj=dj):
                        return s.activation(out=abel_bf[:, dj * TC:(dj + 1) * TC], in_=ps[:, :],
                                            func=AF.Copy)
                    mm_ev(1, [fps0 if dj % 2 == 0 else fps1], abm, abe,
                          head_pe=[t_wl, t_h1],
                          head_ev=[prev_st4, (h_sdma, sdma.n)] if dj == 0 else None)
                    wb_free[sl] = pe.tok()
                t_abelbf = act.tok()

                if "abel" in taps and kf == 0:
                    def tapab(s, tk=t_abelbf):
                        wait(s, tk)
                        i = s.dma_start(out=taps["abel"].ap()[:, :], in_=abel_bf[:, :])
                        sdma.inc(i, 16)
                    block.sync(tapab)

                def flg(t, t_xk=t_xk):
                    wait(t, t_xk)
                    pp.prod_wait(t, lps)
                    for dk in range(DK):
                        t.matmul(lps[:, :], gatewc[:, dk * NE:(dk + 1) * NE],
                                 xk[:, dk * TC:(dk + 1) * TC],
                                 start=(dk == 0), stop=False)
                    mm = t.matmul(lps[:, :], gatebrow[0:1, :], ones_tc[:, :],
                                  start=False, stop=True)
                    pe.inc(mm)
                block.tensor(flg)
                t_lg = pe.tok()

                def flge(s, t_lg=t_lg):
                    wait(s, t_lg)
                    wait(s, (h_sdma, sdma.n))
                    i = s.activation(out=lgrow[:, :], in_=lps[:, :], func=AF.Copy)
                    act.inc(i)
                block.scalar(flge)
                t_lgr = act.tok()
                pp.consumed(lps, t_lgr)

                def flgt(t, t_lgr=t_lgr):
                    wait(t, t_lgr)
                    pp.prod_wait(t, gps0)
                    mm = None
                    for j in range(4):
                        mm = t.transpose(bass.AP(gps0, j * NE, [[TC, 128], [1, NE]]),
                                         lgrow[0:NE, j * 128:(j + 1) * 128], ident_f[0:NE, 0:NE])
                    pe.inc(mm)
                block.tensor(flgt)
                tk = pe.tok()

                def flgte(s, tk=tk):
                    wait(s, tk)
                    i = s.activation(out=lgT[:, 0:24], in_=bass.AP(gps0, 0, [[TC, 128], [1, 24]]),
                                     func=AF.Copy)
                    act.inc(i)
                block.scalar(flgte)
                t_lgT = act.tok()
                pp.consumed(gps0, t_lgT)

                def ftop1(v, t_lgT=t_lgT):
                    wait(v, t_lgT)
                    i = None
                    for j in range(4):
                        i = v.tensor_reduce(out=ttiny[:, j:j + 1], in_=lgT[:, j * 6:(j + 1) * 6],
                                            axis=mybir.AxisListType.X, op=OP.max)
                    dve.inc(i)
                block.vector(ftop1)
                tk = dve.tok()

                def ftop2(v, tk=tk):
                    wait(v, tk)
                    i = None
                    for j in range(4):
                        i = v.tensor_scalar(out=wscr[:, j * 6:(j + 1) * 6],
                                            in0=lgT[:, j * 6:(j + 1) * 6],
                                            scalar1=ttiny[:, j:j + 1], scalar2=None,
                                            op0=OP.is_equal)
                    dve.inc(i)
                block.vector(ftop2)
                tk = dve.tok()

                def ftop3(v, tk=tk):
                    wait(v, tk)
                    i = None
                    for j in range(4):
                        i = v.scalar_tensor_tensor(out=wscr[:, 24 + j * 6:24 + (j + 1) * 6],
                                                   in0=wscr[:, j * 6:(j + 1) * 6], scalar=-1e30,
                                                   in1=lgT[:, j * 6:(j + 1) * 6],
                                                   op0=OP.mult, op1=OP.add)
                    dve.inc(i)
                block.vector(ftop3)
                tk = dve.tok()

                def ftop4(v, tk=tk):
                    wait(v, tk)
                    i = None
                    for j in range(4):
                        i = v.tensor_reduce(out=ttiny[:, 4 + j:5 + j],
                                            in_=wscr[:, 24 + j * 6:24 + (j + 1) * 6],
                                            axis=mybir.AxisListType.X, op=OP.max)
                    dve.inc(i)
                block.vector(ftop4)
                tk = dve.tok()

                def ftop5(v, tk=tk):
                    wait(v, tk)
                    i = v.tensor_tensor(out=ttiny[:, 8:12], in0=ttiny[:, 4:8], in1=ttiny[:, 0:4],
                                        op=OP.subtract)
                    dve.inc(i)
                block.vector(ftop5)
                tk = dve.tok()

                def ftop6(s, tk=tk):
                    wait(s, tk)
                    i = s.activation(out=ttiny[:, 12:16], in_=ttiny[:, 8:12], func=AF.Exp)
                    act.inc(i)
                block.scalar(ftop6)
                tk = act.tok()

                def ftop7(v, tk=tk):
                    wait(v, tk)
                    i = v.tensor_scalar(out=ttiny[:, 16:20], in0=ttiny[:, 12:16], scalar1=1.0,
                                        scalar2=None, op0=OP.add)
                    dve.inc(i)
                block.vector(ftop7)
                tk = dve.tok()

                def ftop8(v, tk=tk):
                    wait(v, tk)
                    i = v.reciprocal(out=ttiny[:, 20:24], in_=ttiny[:, 16:20])
                    dve.inc(i)
                block.vector(ftop8)
                tk = dve.tok()

                def ftop9(v, tk=tk):
                    wait(v, tk)
                    i = v.tensor_tensor(out=ttiny[:, 24:28], in0=ttiny[:, 12:16],
                                        in1=ttiny[:, 20:24], op=OP.mult)
                    dve.inc(i)
                block.vector(ftop9)
                tk = dve.tok()

                def ftop10(v, tk=tk):
                    wait(v, tk)
                    i = None
                    for j in range(4):
                        i = v.tensor_scalar(out=wscr[:, 48 + j * 6:48 + (j + 1) * 6],
                                            in0=wscr[:, 24 + j * 6:24 + (j + 1) * 6],
                                            scalar1=ttiny[:, 4 + j:5 + j], scalar2=None,
                                            op0=OP.is_equal)
                    dve.inc(i)
                block.vector(ftop10)
                tk = dve.tok()

                def ftop11(v, tk=tk):
                    wait(v, tk)
                    i = None
                    for j in range(4):
                        i = v.tensor_scalar(out=wscr[:, 48 + j * 6:48 + (j + 1) * 6],
                                            in0=wscr[:, 48 + j * 6:48 + (j + 1) * 6],
                                            scalar1=ttiny[:, 24 + j:25 + j], scalar2=None,
                                            op0=OP.mult)
                    dve.inc(i)
                block.vector(ftop11)
                tk = dve.tok()

                def ftop12(v, tk=tk):
                    wait(v, tk)
                    last = None
                    for j in range(4):
                        last = v.scalar_tensor_tensor(out=wscr[:, 72 + j * 6:72 + (j + 1) * 6],
                                                      in0=wscr[:, j * 6:(j + 1) * 6],
                                                      scalar=ttiny[:, 20 + j:21 + j],
                                                      in1=wscr[:, 48 + j * 6:48 + (j + 1) * 6],
                                                      op0=OP.mult, op1=OP.add)
                    dve.inc(last)
                block.vector(ftop12)
                t_wfull = dve.tok()

                def fwt(t, t_wfull=t_wfull):
                    wait(t, t_wfull)
                    pp.prod_wait(t, wtp)
                    mm = None
                    for j in range(4):
                        mm = t.transpose(wtp[0:NE, j * 128:(j + 1) * 128],
                                         wscr[:, 72 + j * 6:72 + (j + 1) * 6], ident_f[:, :])
                    pe.inc(mm)
                block.tensor(fwt)
                tk = pe.tok()

                def fwte(s, tk=tk):
                    wait(s, tk)
                    i = s.activation(out=wT6[:, :], in_=wtp[0:NE, :], func=AF.Copy)
                    act.inc(i)
                block.scalar(fwte)
                t_wT6 = act.tok()
                pp.consumed(wtp, t_wT6)

                if "wT6" in taps and kf == 0:
                    def tapw(s, tk=t_wT6):
                        wait(s, tk)
                        i = s.dma_start(out=taps["wT6"].ap()[0:NE, :], in_=wT6[:, :])
                        sdma.inc(i, 16)
                    block.sync(tapw)

                for e in range(NE):
                    def fwb(t, e=e, t_wT6=t_wT6):
                        wait(t, t_wT6)
                        if e == 0:
                            wait(t, t_sel)
                        ps = gps0 if e % 2 == 0 else gps1
                        pp.prod_wait(t, ps)
                        i = t.matmul(ps[:, :], selrow[0:NE, e * 128:(e + 1) * 128], wT6[:, :],
                                     start=True, stop=True)
                        pe.inc(i)
                        g_toks1[32 + e] = pe.tok()
                    block.tensor(fwb)

                    def fwbe(s, e=e):
                        wait(s, g_toks1[32 + e])
                        ps = gps0 if e % 2 == 0 else gps1
                        i = s.activation(out=wbc[:, e * TC:(e + 1) * TC], in_=ps[:, :], func=AF.Copy)
                        act.inc(i)
                        pp.consumed(ps, act.tok())
                    block.scalar(fwbe)
                t_wbc = act.tok()

                def a0m(t, ps, dj):
                    return t.matmul(ps[:, :], eb2r[0:NE, dj * 128:(dj + 1) * 128], wT6[:, :],
                                    start=True, stop=True)

                def a0e(s, ps, dj):
                    return s.activation(out=accf[:, dj * TC:(dj + 1) * TC], in_=ps[:, :],
                                        func=AF.Copy)
                t_acc = mm_ev(DK, [fps0, fps1], a0m, a0e,
                              head_pe=[t_wT6, t_eb2],
                              head_ev=[t_moe_prev, (h_sdma, sdma.n)])

                for e in range(NE):
                    def febrow(g, e=e, tprev=(h_pe, pe.n), tw=t_wcast):
                        wait(g, tprev)
                        wait(g, tw)
                        ebs.inc(g.dma_start(out=eb1row[:, :],
                                            in_=s_eb1.ap()[0:1, e * HM:(e + 1) * HM]), 16)
                    block.gpsimd(febrow)
                    t_ebrow = ebs.tok()
                    for hb in range(4):
                        wtile, sl, t_wl = stream_w(
                            s_ew1.ap()[e, hb * 128:(hb + 1) * 128, :], 4096)

                        def hem(t, ps, sub, wtile=wtile, hb=hb):
                            hc = hb * 4 + sub
                            for dk in range(DK):
                                t.matmul(ps[:, :],
                                         wtile[:, dk * 512 + sub * 128:dk * 512 + (sub + 1) * 128],
                                         xk[:, dk * TC:(dk + 1) * TC],
                                         start=(dk == 0), stop=False)
                            return t.matmul(ps[:, :], eb1row[0:1, hc * 128:(hc + 1) * 128],
                                            ones_tc[:, :], start=False, stop=True)

                        def hee(s, ps, sub, hb=hb, e=e):
                            hc = hb * 4 + sub
                            return s.activation(
                                out=hbuf[:, hc * TC:(hc + 1) * TC], in_=ps[:, :],
                                func=AF.Gelu)
                        mm_ev(4, [fps0, fps1], hem, hee, head_pe=[t_wl, t_xk, t_ebrow])
                        wb_free[sl] = pe.tok()
                    t_he = act.tok()

                    for dj in range(DK):
                        wtile, sl, t_wl = stream_w(
                            s_ew2.ap()[e, dj * 128:(dj + 1) * 128, :], 2048)

                        def feo(t, dj=dj, wtile=wtile, t_wl=t_wl, t_he=t_he):
                            wait(t, t_wl)
                            wait(t, t_he)
                            ps = fps0 if dj % 2 == 0 else fps1
                            pp.prod_wait(t, ps)
                            mm = None
                            for kc in range(16):
                                mm = t.matmul(ps[:, :], wtile[:, kc * 128:(kc + 1) * 128],
                                              hbuf[:, kc * TC:(kc + 1) * TC],
                                              start=(kc == 0), stop=(kc == 15))
                            pe.inc(mm)
                            g_toks2[16 + dj] = pe.tok()
                        block.tensor(feo)
                        wb_free[sl] = pe.tok()

                        def feos(v, dj=dj, e=e, t_wbc=t_wbc):
                            wait(v, g_toks2[16 + dj])
                            wait(v, t_wbc)
                            ps = fps0 if dj % 2 == 0 else fps1
                            i = v.tensor_tensor(out=t32b[:, dj * TC:(dj + 1) * TC], in0=ps[:, :],
                                                in1=wbc[:, e * TC:(e + 1) * TC], op=OP.mult)
                            dve.inc(i)
                            pp.consumed(ps, dve.tok())
                            g_toks2[32 + dj] = dve.tok()
                        block.vector(feos)

                        def feoa(g, dj=dj, e=e, t_acc=t_acc):
                            wait(g, g_toks2[32 + dj])
                            if e == 0:
                                wait(g, t_acc)
                            i = g.tensor_tensor(out=accf[:, dj * TC:(dj + 1) * TC],
                                                in0=accf[:, dj * TC:(dj + 1) * TC],
                                                in1=t32b[:, dj * TC:(dj + 1) * TC], op=OP.add)
                            pool.inc(i)
                        block.gpsimd(feoa)
                t_moe = pool.tok()

                if "acc" in taps and kf == 0:
                    def tapacc(s, tk=t_moe):
                        wait(s, tk)
                        i = s.dma_start(out=taps["acc"].ap()[:, :], in_=accf[:, :])
                        sdma.inc(i, 16)
                    block.sync(tapacc)

                def fmc(v, t_moe=t_moe):
                    wait(v, t_moe)
                    wait(v, (h_sdma, sdma.n))
                    i = v.tensor_copy(out=nonab_bf[:, :], in_=accf[:, :])
                    dve.inc(i)
                block.vector(fmc)
                t_accbf = dve.tok()
                t_moe_prev = t_accbf

                def rtm(t, ps, dj):
                    mm = None
                    for kc in range(DK):
                        mm = t.matmul(ps[:, :],
                                      Rmat[:, kc * D + dj * 128:kc * D + (dj + 1) * 128],
                                      nonab_bf[:, kc * TC:(kc + 1) * TC],
                                      start=(kc == 0), stop=(kc == DK - 1))
                    return mm

                def rte(s, ps, dj):
                    return s.activation(out=t32a[:, dj * TC:(dj + 1) * TC], in_=ps[:, :],
                                        func=AF.Copy)
                t_rot = mm_ev(DK, [fps0, fps1], rtm, rte, head_pe=[t_accbf, t_R])

                def frotbf(v, t_rot=t_rot):
                    wait(v, t_rot)
                    i = v.tensor_copy(out=nonab_bf[:, :], in_=t32a[:, :])
                    dve.inc(i)
                block.vector(frotbf)
                t_nonbf = dve.tok()

                if "nonab" in taps and kf == 0:
                    def tapnb(s, tk=t_nonbf):
                        wait(s, tk)
                        i = s.dma_start(out=taps["nonab"].ap()[:, :], in_=nonab_bf[:, :])
                        sdma.inc(i, 16)
                    block.sync(tapnb)

                def fgc(t, t_abelbf=t_abelbf, t_nonbf=t_nonbf):
                    wait(t, t_abelbf)
                    wait(t, t_nonbf)
                    pp.prod_wait(t, rps)
                    mm = None
                    for dk in range(DK):
                        mm = t.matmul(rps[:, :], gcwc[:, dk:dk + 1],
                                      abel_bf[:, dk * TC:(dk + 1) * TC],
                                      start=(dk == 0), stop=False)
                    for dk in range(DK):
                        mm = t.matmul(rps[:, :], gcwc[:, DK + dk:DK + dk + 1],
                                      nonab_bf[:, dk * TC:(dk + 1) * TC],
                                      start=False, stop=False)
                    mm = t.matmul(rps[:, :], gcbrow[0:1, :], ones_tc[:, :],
                                  start=False, stop=True)
                    pe.inc(mm)
                block.tensor(fgc)
                t_gc = pe.tok()

                def fgce(s, t_gc=t_gc):
                    wait(s, t_gc)
                    i = s.activation(out=rows[0:1, 2 * TC:3 * TC], in_=rps[:, :], func=AF.Sigmoid)
                    act.inc(i)
                block.scalar(fgce)
                t_sig = act.tok()
                pp.consumed(rps, t_sig)

                def fgb(t, t_sig=t_sig):
                    wait(t, t_sig)
                    pp.prod_wait(t, gps0)
                    i = t.matmul(gps0[:, :], ones_f_row[:, :], rows[0:1, 2 * TC:3 * TC],
                                 start=True, stop=True)
                    pe.inc(i)
                block.tensor(fgb)
                tk = pe.tok()

                def fgbe(s, tk=tk):
                    wait(s, tk)
                    i = s.activation(out=fbct[:, 2 * TC:3 * TC], in_=gps0[:, :], func=AF.Copy)
                    act.inc(i)
                block.scalar(fgbe)
                t_gb = act.tok()
                pp.consumed(gps0, t_gb)

                def fmx1(v, t_gb=t_gb):
                    wait(v, t_gb)
                    wait(v, (h_sdma, sdma.n))
                    i = None
                    for dk in range(DK):
                        i = v.tensor_tensor(out=t32b[:, dk * TC:(dk + 1) * TC],
                                            in0=nonab_bf[:, dk * TC:(dk + 1) * TC],
                                            in1=abel_bf[:, dk * TC:(dk + 1) * TC], op=OP.subtract)
                    dve.inc(i)
                block.vector(fmx1)
                tk = dve.tok()

                def fmx2(v, tk=tk):
                    wait(v, tk)
                    i = None
                    for dk in range(DK):
                        i = v.tensor_tensor(out=t32b[:, dk * TC:(dk + 1) * TC],
                                            in0=t32b[:, dk * TC:(dk + 1) * TC],
                                            in1=fbct[:, 2 * TC:3 * TC], op=OP.mult)
                    dve.inc(i)
                block.vector(fmx2)
                tk = dve.tok()

                def fmx3(v, tk=tk):
                    wait(v, tk)
                    last = None
                    for dk in range(DK):
                        last = v.tensor_tensor(out=t32b[:, dk * TC:(dk + 1) * TC],
                                               in0=t32b[:, dk * TC:(dk + 1) * TC],
                                               in1=abel_bf[:, dk * TC:(dk + 1) * TC], op=OP.add)
                    dve.inc(last)
                block.vector(fmx3)
                t_mix = dve.tok()

                def fn2l(s, kf=kf, tdve=(h_dve, dve.n)):
                    wait(s, tdve)
                    i = s.dma_start(out=ldbuf[:, 0:DK * TC],
                                    in_=bass.AP(d_n2, kf * 128 * DKTC, [[DKTC, 128], [1, DKTC]]))
                    ldma.inc(i, 16)
                block.sync(fn2l)
                t_n2l = ldma.tok()

                def ff(v, t_mix=t_mix, t_n2l=t_n2l):
                    wait(v, t_mix)
                    wait(v, t_n2l)
                    last = None
                    for dk in range(DK):
                        last = v.tensor_tensor(out=t32b[:, dk * TC:(dk + 1) * TC],
                                               in0=t32b[:, dk * TC:(dk + 1) * TC],
                                               in1=ldbuf[:, dk * TC:(dk + 1) * TC], op=OP.add)
                    dve.inc(last)
                block.vector(ff)
                t_f = dve.tok()

                if "f" in taps and kf == 0:
                    def tapf(s, tk=t_f):
                        wait(s, tk)
                        i = s.dma_start(out=taps["f"].ap()[:, :], in_=t32b[:, :])
                        sdma.inc(i, 16)
                    block.sync(tapf)

                t_fbf = copy_sbf(t32b[:, :], t_f)
                tk_st4 = ln_stats(t_fbf)
                prev_st4 = tk_st4

                def fx1l(s, kf=kf, tdve=(h_dve, dve.n)):
                    wait(s, tdve)
                    i = s.dma_start(out=ldbuf[:, 0:DK * TC],
                                    in_=bass.AP(d_x1, kf * 128 * DKTC, [[DKTC, 128], [1, DKTC]]))
                    ldma.inc(i, 16)
                block.sync(fx1l)
                t_x1l = ldma.tok()

                t_x2 = normalize(t32b, t32b, tk_st4,
                                 extra=ldbuf[:, 0:DK * TC], t_extra=t_x1l)

                def fst(s, kf=kf, t_x2=t_x2):
                    wait(s, t_x2)
                    i = s.dma_start(out=bass.AP(d_out, kf * 128 * DKTC, [[DKTC, 128], [1, DKTC]]),
                                    in_=t32b[:, :])
                    sdma.inc(i, 16)
                block.sync(fst)
                prev_xn2_pe = pe.tok()

            t_final = sdma.tok()

            def ffin(s, t_final=t_final):
                wait(s, t_final)
            block.sync(ffin)

    return nc


def _tmaj(v, nch):
    T_, Dd = v.shape
    return np.ascontiguousarray(
        v.reshape(nch, TC, Dd // 128, 128).transpose(0, 3, 2, 1).reshape(nch * 128, -1))


def _wmaj(w, bs):
    K_, M_ = w.shape
    return np.ascontiguousarray(
        w.reshape(K_ // 128, 128, M_ // bs, bs).transpose(2, 1, 0, 3)
        .reshape((M_ // bs) * 128, (K_ // 128) * bs))


def _rmaj(w):
    K_, M_ = w.shape
    return np.ascontiguousarray(
        w.reshape(K_ // 128, 128, M_).transpose(1, 0, 2).reshape(128, -1))


def kernel(**inputs):
    cfg = CFG
    x = np.asarray(inputs["L0_attn_x"])
    W_in = np.asarray(inputs["L0_attn_W_in"])
    conv_w = np.asarray(inputs["L0_attn_conv_w"])
    conv_b = np.asarray(inputs["L0_attn_conv_b"])
    A_log = np.asarray(inputs["L0_attn_A_log"])
    D_param = np.asarray(inputs["L0_attn_D_param"])
    W_out = np.asarray(inputs["L0_attn_W_out"])

    nc = build_program(cfg)
    NCA, NCF = cfg["NCA"], cfg["NCF"]
    aw1_t = _wmaj(np.asarray(inputs["L0_ffn_aw1"]), 512)
    aw2_t = _wmaj(np.asarray(inputs["L0_ffn_aw2"]), 128)
    ew1_t = np.ascontiguousarray(np.stack(
        [_wmaj(np.asarray(inputs["L0_ffn_ew1"])[e], 512) for e in range(NE)]))
    ew2_t = np.ascontiguousarray(np.stack(
        [_wmaj(np.asarray(inputs["L0_ffn_ew2"])[e], 128) for e in range(NE)]))
    gatew_t = _rmaj(np.asarray(inputs["L0_ffn_gate_w"]))
    in_maps = []
    for c in range(8):
        b, j = c // 2, c % 2
        m = {
            "xT": _tmaj(x[b], NCA),
            "x0T": _tmaj(x[b][j * 4096:(j + 1) * 4096], NCF),
            "win": _rmaj(np.concatenate(
                [W_in[:, j * 1024:(j + 1) * 1024],
                 W_in[:, 2048 + j * 1024:2048 + (j + 1) * 1024]], axis=1)),
            "wout": _rmaj(W_out[j * 1024:(j + 1) * 1024, :]),
            "convw": _rmaj(conv_w[j * 1024:(j + 1) * 1024, 0, :]),
            "convb": np.ascontiguousarray(conv_b[None, j * 1024:(j + 1) * 1024]),
            "alog": np.ascontiguousarray(A_log[:, j * 1024:(j + 1) * 1024]),
            "dp": np.ascontiguousarray(D_param[None, j * 1024:(j + 1) * 1024]),
            "lng": np.ascontiguousarray(np.asarray(inputs["L0_ffn_ln_g"])[None, :]),
            "lnb": np.ascontiguousarray(np.asarray(inputs["L0_ffn_ln_b"])[None, :]),
            "aw1": aw1_t,
            "ab1": np.ascontiguousarray(np.asarray(inputs["L0_ffn_ab1"])[None, :]),
            "aw2": aw2_t,
            "ab2": np.ascontiguousarray(np.asarray(inputs["L0_ffn_ab2"])[None, :]),
            "gatew": gatew_t,
            "gateb": np.ascontiguousarray(np.asarray(inputs["L0_ffn_gate_b"])[None, :]),
            "ew1": ew1_t,
            "eb1": np.ascontiguousarray(np.asarray(inputs["L0_ffn_eb1"]).reshape(1, -1)),
            "ew2": ew2_t,
            "eb2": np.asarray(inputs["L0_ffn_eb2"]),
            "gens": _rmaj(np.asarray(inputs["L0_ffn_gens"])[c % 4]),
            "gcw": np.ascontiguousarray(np.asarray(inputs["L0_ffn_gc_w"]).reshape(1, -1)),
            "gcb": np.ascontiguousarray(np.asarray(inputs["L0_ffn_gc_b"]).reshape(1, 1)),
        }
        in_maps.append(m)

    res = run_bass_kernel_spmd(nc, in_maps, core_ids=list(range(8)))
    out = np.empty((4, 8192, 1024), np.float32)
    for c in range(8):
        b, j = c // 2, c % 2
        r = np.asarray(res.results[c]["x2T"])
        out[b, j * 4096:(j + 1) * 4096, :] = (
            r.reshape(NCF, 128, DK, TC).transpose(0, 3, 2, 1).reshape(NCF * TC, D))
    return out
